# version 31
# baseline (speedup 1.0000x reference)
import contextlib
import numpy as np

import concourse.bass as bass
import concourse.mybir as mybir
from concourse.bass_utils import run_bass_kernel_spmd

F32 = mybir.dt.float32
BF16 = mybir.dt.bfloat16
AF = mybir.ActivationFunctionType
ALU = mybir.AluOpType
AX = mybir.AxisListType

P = 128
D = 1024
DCH = D // P
E = 8
FF = 4096
HC = FF // P
HCG = 4
NHCG = HC // HCG
HD = 256
NHEAD = 4
EPS = 1e-5
K41, K21 = 168, 148
PH41, PH21 = K41 + 1, K21 + 1


class Buf:
    __slots__ = ("name", "w", "readers")

    def __init__(self, name):
        self.name = name
        self.w = None
        self.readers = []


class Ctx:
    def __init__(self, nc, stack):
        self.nc = nc
        self.stack = stack
        self.sems = {}
        self.counts = {}
        self.q = {"tensor": nc.tensor, "vector": nc.vector, "scalar": nc.scalar,
                  "gpsimd": nc.gpsimd, "sync": nc.sync}
        self.waited = {k: {} for k in self.q}
        for k in ("tensor", "vector", "scalar", "gpsimd"):
            self._mksem("e_" + k)

    def _mksem(self, name):
        self.sems[name] = self.stack.enter_context(self.nc.semaphore(name))
        self.counts[name] = 0

    def _wait(self, eng, tok):
        if tok is None:
            return
        sem_name, val = tok
        wm = self.waited[eng]
        if wm.get(sem_name, 0) >= val:
            return
        self.q[eng].wait_ge(self.sems[sem_name], val)
        wm[sem_name] = val

    def _edges(self, eng, reads, writes):
        for b in reads:
            self._wait(eng, b.w)
        for b in writes:
            self._wait(eng, b.w)
            for t in b.readers:
                self._wait(eng, t)

    def _record(self, tok, reads, writes):
        for b in reads:
            b.readers.append(tok)
        for b in writes:
            b.w = tok
            b.readers = []

    def step(self, eng, reads, writes, fn):
        self._edges(eng, reads, writes)
        last = fn(self.q[eng])
        sem_name = "e_" + eng
        self.counts[sem_name] += 1
        last.then_inc(self.sems[sem_name], 1)
        self._record((sem_name, self.counts[sem_name]), reads, writes)

    def dma(self, qeng, sem_name, reads, writes, fn):
        sem_name = "d_" + writes[0].name
        if sem_name not in self.sems:
            self._mksem(sem_name)
        self._edges(qeng, reads, writes)
        inst = fn(self.q[qeng])
        self.counts[sem_name] += 16
        inst.then_inc(self.sems[sem_name], 16)
        self._record((sem_name, self.counts[sem_name]), reads, writes)

    def final_wait(self, eng, sem_name):
        if sem_name in self.sems:
            self.q[eng].wait_ge(self.sems[sem_name], self.counts[sem_name])
        else:
            for sname, val in self.counts.items():
                if val > 0:
                    self.q[eng].wait_ge(self.sems[sname], val)

    def barrier(self):
        for eng in self.q:
            wm = self.waited[eng]
            for sname, val in self.counts.items():
                if val > 0 and wm.get(sname, 0) < val:
                    self.q[eng].wait_ge(self.sems[sname], val)
                    wm[sname] = val


def build_program(NT=16, GRP_T=8, phase=None):
    TCORE = NT * P
    NGRP = NT // GRP_T
    assert NT % GRP_T == 0
    GT = GRP_T * P
    NTH = max(1, GT // 512)
    THW = GT // NTH
    NSEG = max(1, GT // 512)
    SEGW = GT // NSEG

    nc = bass.Bass("TRN2", target_bir_lowering=False)

    xh = nc.dram_tensor("xh", [(NT + 2) * P, D], F32, kind="ExternalInput").ap()
    lrmask = nc.dram_tensor("lrmask", [P, 2], F32, kind="ExternalInput").ap()
    w_qv = nc.dram_tensor("w_qv", [D, 2 * D], F32, kind="ExternalInput").ap()
    w2c = nc.dram_tensor("w2c", [D, 248], F32, kind="ExternalInput").ap()
    swin = nc.dram_tensor("swin", [1, 2], F32, kind="ExternalInput").ap()
    w_out = nc.dram_tensor("w_out", [D, D], F32, kind="ExternalInput").ap()
    gamma = nc.dram_tensor("gamma", [1, D], F32, kind="ExternalInput").ap()
    beta = nc.dram_tensor("beta", [1, D], F32, kind="ExternalInput").ap()
    wr = nc.dram_tensor("wr", [D, E], F32, kind="ExternalInput").ap()
    brr = nc.dram_tensor("brr", [1, E], F32, kind="ExternalInput").ap()
    w1h = nc.dram_tensor("w1h", [E, NHCG, P, DCH * HCG * P], F32, kind="ExternalInput").ap()
    b1 = nc.dram_tensor("b1", [P, E * HC], F32, kind="ExternalInput").ap()
    w2 = nc.dram_tensor("w2", [E, FF, D], F32, kind="ExternalInput").ap()
    b2 = nc.dram_tensor("b2", [E, D], F32, kind="ExternalInput").ap()
    cvec = nc.dram_tensor("cvec", [1, D], F32, kind="ExternalInput").ap()
    wmod = nc.dram_tensor("wmod", [D, 6 * D], F32, kind="ExternalInput").ap()
    y = nc.dram_tensor("y", [TCORE, D], F32, kind="ExternalOutput").ap()

    fb41 = [nc.dram_tensor(f"fb41_{j}", [NHEAD * P * PH41 + 64], BF16, kind="Internal")
            for j in range(2)]
    fb21 = [nc.dram_tensor(f"fb21_{j}", [NHEAD * P * PH21 + 64], BF16, kind="Internal")
            for j in range(2)]
    w2b = nc.dram_tensor("w2b", [E, FF, D], BF16, kind="Internal").ap()
    x2h = nc.dram_tensor("x2h", [TCORE, D], F32, kind="Internal").ap()
    h2h = nc.dram_tensor("h2h", [TCORE, D], BF16, kind="Internal").ap()
    xnth = nc.dram_tensor("xnth", [DCH, P, TCORE], BF16, kind="Internal").ap()
    hrow = nc.dram_tensor("hrow", [1, 6 * D], F32, kind="Internal").ap()
    wshf = nc.dram_tensor("wshf", [16, TCORE], F32, kind="Internal").ap()
    wshb = nc.dram_tensor("wshb", [16, TCORE], BF16, kind="Internal").ap()

    with contextlib.ExitStack() as stack:
        c = Ctx(nc, stack)
        sb = lambda n, s, dt=F32: stack.enter_context(nc.sbuf_tensor(n, s, dt))
        psd = lambda n, s, dt=F32: stack.enter_context(nc.psum_tensor(n, s, dt))

        pA = psd("pA", [P, 1024]); bpA = Buf("pA")
        pB = psd("pB", [P, 1024]); bpB = Buf("pB")
        pC = psd("pC", [P, 1024]); bpC = Buf("pC")
        pS = psd("pS", [P, 512]); bpS = Buf("pS")
        pT = psd("pT", [P, 512]); bpT = Buf("pT")
        bpChi = Buf("pChi")
        mm2bank = [(pA, 0, bpA), (pA, 512, bpA), (pB, 0, bpB), (pB, 512, bpB),
                   (pC, 0, bpC), (pC, 512, bpC), (pS, 0, bpS), (pT, 0, bpT)]

        ident = sb("ident", [P, P]); bident = Buf("ident")
        eps_t = sb("eps_t", [P, 1])
        ones1 = sb("ones1", [1, P])
        ones1b = sb("ones1b", [1, P], BF16)
        bconst = Buf("const")
        rowt = sb("rowt", [1, D]); browt = Buf("rowt")
        row2 = sb("row2", [1, D]); brow2 = Buf("row2")
        bxnth = Buf("xnth")
        bw2b = Buf("w2b")
        bwshf, bwshb = Buf("wshf"), Buf("wshb")
        bc9 = sb("bc9", [16, D], BF16); bbc9 = Buf("bc9")
        ctrow = sb("ctrow", [1, D]); bctrow = Buf("ctrow")
        b1t = sb("b1t", [P, E * HC]); bb1t = Buf("b1t")
        rb1t = sb("rb1t", [P, E * HC], BF16); brb1t = Buf("rb1t")
        ones8 = sb("ones8", [8, 1])
        ohot = sb("ohot", [16, E * P]); bohot = Buf("ohot")
        gb_r = sb("gb_r", [P, D]); bgb = Buf("gb_r")
        bx2h, bh2h, bxh, by = Buf("x2h"), Buf("h2h"), Buf("xh"), Buf("y")
        bfb41 = [Buf(f"fb41_{j}") for j in range(2)]
        bfb21 = [Buf(f"fb21_{j}") for j in range(2)]

        c.step("gpsimd", [], [bident], lambda q: q.memset(ident[:], 0.0))
        c.step("gpsimd", [], [bident], lambda q: q.affine_select(
            out=ident[:], in_=ident[:], compare_op=ALU.not_equal, fill=1.0,
            base=0, pattern=[[-1, P]], channel_multiplier=1))
        c.step("gpsimd", [], [bconst], lambda q: q.memset(eps_t[:], EPS))
        c.step("gpsimd", [], [bconst], lambda q: q.memset(ones1[:], 1.0))
        c.step("gpsimd", [], [bconst], lambda q: q.memset(ones1b[:], 1.0))
        c.step("gpsimd", [], [bctrow], lambda q: q.memset(ctrow[:], 0.0))
        c.step("gpsimd", [], [bconst], lambda q: q.memset(ones8[:], 1.0))
        c.step("gpsimd", [], [bohot], lambda q: q.memset(ohot[:], 1.0))
        c.step("gpsimd", [], [bohot], lambda q: q.affine_select(
            out=ohot[:], in_=ohot[:], compare_op=ALU.is_equal, fill=0.0,
            base=0, pattern=[[-1, E], [0, P]], channel_multiplier=1))

        def bcast_seq(src_t, src_off, n, dst_t, bdst, bsrc, scale_add=None):
            for j in range((n + 511) // 512):
                w = min(512, n - 512 * j)
                c.step("tensor", [bsrc, bconst], [bpS],
                       lambda q, j=j, w=w: q.matmul(
                           pS[:, 0:w], ones1[:],
                           bass.AP(src_t.ap().tensor, src_off + 512 * j,
                                   [[src_t.shape[1], 1], [1, w]]),
                           start=True, stop=True))
                if scale_add is None:
                    c.step("vector", [bpS], [bdst],
                           lambda q, j=j, w=w: q.tensor_copy(
                               out=dst_t[:, 512 * j:512 * j + w], in_=pS[:, 0:w]))
                else:
                    c.step("vector", [bpS], [bdst],
                           lambda q, j=j, w=w: q.tensor_scalar(
                               out=dst_t[:, 512 * j:512 * j + w], in0=pS[:, 0:w],
                               scalar1=scale_add, scalar2=None, op0=ALU.add))

        with contextlib.ExitStack() as pa_stack:
            sa_ = lambda n, s, dt=F32: pa_stack.enter_context(nc.sbuf_tensor(n, s, dt))
            zb41 = sa_("zb41", [P, 4 * PH41], BF16); bzb = Buf("zb41")
            masks = sa_("masks", [P, 2]); bmasks = Buf("masks")
            modc = sa_("modc", [P, 2 * DCH]); bmodc = Buf("modc")
            modcb = sa_("modcb", [P, 2 * DCH], BF16); bmodcb = Buf("modcb")
            cb1_r = sa_("cb1_r", [P, D]); bcb1 = Buf("cb1_r")
            sb_r = sa_("sb_r", [P, D]); bsb = Buf("sb_r")
            gam_r = sa_("gam_r", [P, D]); bgam = Buf("gam_r")
            bet_r = sa_("bet_r", [P, D]); bbet = Buf("bet_r")
            br_r = sa_("br_r", [P, E]); bbrr = Buf("br_r")
            swb = sa_("swb", [P, 2]); bswb = Buf("swb")
            qvb = sa_("qvb", [1, 2 * D], BF16); bqvb = Buf("qvb")
            wqv_s = sa_("wqv_s", [P, DCH * 2 * D], BF16); bwqv = Buf("wqv_s")
            w2c_s = sa_("w2c_s", [P, DCH * 248], BF16); bw2c = Buf("w2c_s")
            wout_s = sa_("wout_s", [P, DCH * D], BF16); bwout = Buf("wout_s")
            wr_s = sa_("wr_s", [P, DCH * E], BF16); bwr = Buf("wr_s")

            c.step("gpsimd", [], [bzb], lambda q: q.memset(zb41[:], 0.0))
            c.dma("sync", "d_misc", [], [bmasks],
                  lambda q: q.dma_start(out=masks[:], in_=lrmask))
            for j in range(2):
                c.dma("sync", "d_band0", [bzb], [bfb41[j]], lambda q, j=j: q.dma_start(
                    out=bass.AP(fb41[j], 0, [[4 * PH41, P], [1, 4 * PH41]]), in_=zb41[:]))
                c.dma("sync", "d_band0", [bzb], [bfb21[j]], lambda q, j=j: q.dma_start(
                    out=bass.AP(fb21[j], 0, [[4 * PH21, P], [1, 4 * PH21]]),
                    in_=zb41[:, 0:4 * PH21]))

            p0_stack = contextlib.ExitStack()
            s0_ = lambda n, s, dt=F32: p0_stack.enter_context(nc.sbuf_tensor(n, s, dt))
            scT = s0_("scT", [P, DCH], BF16); bscT = Buf("scT")
            scTf = s0_("scTf", [P, DCH]); bscTf = Buf("scTf")
            wmod_s = s0_("wmod_s", [P, 4 * 512], BF16); bwmod = Buf("wmod_s")
            wmod_f = s0_("wmod_f", [P, 4 * 512]); bwmodf = Buf("wmod_f")
            ga_r = s0_("ga_r", [P, D]); bga = Buf("ga_r")
            gmsk = s0_("gmsk", [1, D]); bgmsk = Buf("gmsk")
            stg512 = s0_("stg512", [1, 512]); bstg = Buf("stg512")
            c.dma("sync", "d_rowt0", [], [browt],
                  lambda q: q.dma_start(out=rowt[0:1, 0:D], in_=cvec))
            c.step("scalar", [browt], [brow2], lambda q: q.activation(
                out=row2[0:1, 0:D], in_=rowt[0:1, 0:D], func=AF.Exp, scale=-1.0))
            c.step("vector", [], [brow2], lambda q: q.tensor_scalar(
                out=row2[0:1, 0:D], in0=row2[0:1, 0:D], scalar1=1.0, scalar2=None,
                op0=ALU.add))
            c.step("vector", [], [brow2], lambda q: q.reciprocal(
                out=row2[0:1, 0:D], in_=row2[0:1, 0:D]))
            c.step("vector", [brow2], [browt], lambda q: q.tensor_tensor(
                out=rowt[0:1, 0:D], in0=rowt[0:1, 0:D], in1=row2[0:1, 0:D], op=ALU.mult))
            bhrow = Buf("hrow")
            c.dma("sync", "d_hrow", [browt], [bhrow], lambda q: q.dma_start(
                out=hrow[0:1, 0:D], in_=rowt[0:1, 0:D]))
            for ch in range(DCH):
                sl = 128 * (ch % 4)
                c.step("tensor", [browt, bident], [bpT],
                       lambda q, ch=ch, sl=sl: q.transpose(
                           pT[0:P, sl:sl + 1], rowt[0:1, ch * P:(ch + 1) * P],
                           ident[0:1, 0:1]))
                c.step("scalar", [bpT], [bscTf],
                       lambda q, ch=ch, sl=sl: q.copy(
                           out=scTf[:, ch:ch + 1], in_=pT[0:P, sl:sl + 1]))
                c.step("vector", [bpT], [bscT],
                       lambda q, ch=ch, sl=sl: q.tensor_copy(
                           out=scT[:, ch:ch + 1], in_=pT[0:P, sl:sl + 1]))
            for j6 in range(12):
                fp32_slot = j6 in (4, 5, 10, 11)
                for half in range(2):
                    if fp32_slot:
                        c.dma("sync", "d_wmodf", [], [bwmodf],
                              lambda q, half=half, j6=j6: q.dma_start(
                                  out=wmod_f.ap().rearrange("p (i f) -> p i f", i=4),
                                  in_=wmod.rearrange("(i p) f -> p i f", p=P)[
                                      :, 4 * half:4 * half + 4, 512 * j6:512 * (j6 + 1)]))
                        def mmodf(q, half=half):
                            for i4 in range(4):
                                last = q.matmul(pS[0:1, 0:512],
                                                scTf[:, 4 * half + i4:4 * half + i4 + 1],
                                                wmod_f[:, 512 * i4:512 * (i4 + 1)],
                                                start=(half == 0 and i4 == 0),
                                                stop=(half == 1 and i4 == 3))
                            return last
                        c.step("tensor", [bscTf, bwmodf], [bpS], mmodf)
                    else:
                        c.dma("gpsimd", "d_wmod", [], [bwmod],
                              lambda q, half=half, j6=j6: q.dma_start(
                                  out=wmod_s.ap().rearrange("p (i f) -> p i f", i=4),
                                  in_=wmod.rearrange("(i p) f -> p i f", p=P)[
                                      :, 4 * half:4 * half + 4, 512 * j6:512 * (j6 + 1)]))
                        def mmod2(q, half=half):
                            for i4 in range(4):
                                last = q.matmul(pS[0:1, 0:512],
                                                scT[:, 4 * half + i4:4 * half + i4 + 1],
                                                wmod_s[:, 512 * i4:512 * (i4 + 1)],
                                                start=(half == 0 and i4 == 0),
                                                stop=(half == 1 and i4 == 3))
                            return last
                        c.step("tensor", [bscT, bwmod], [bpS], mmod2)
                c.step("vector", [bpS], [bstg], lambda q: q.tensor_copy(
                    out=stg512[:], in_=pS[0:1, 0:512]))
                c.dma("sync", "d_hrow", [bstg], [bhrow],
                      lambda q, j6=j6: q.dma_start(
                          out=hrow[0:1, 512 * j6:512 * (j6 + 1)], in_=stg512[:]))
            for j6 in range(4):
                c.dma("sync", "d_rowt0", [bhrow], [browt],
                      lambda q, j6=j6: q.dma_start(
                          out=rowt[0:1, 0:512], in_=hrow[0:1, 512 * j6:512 * (j6 + 1)]))
                for k4 in range(4):
                    col = j6 * 4 + k4
                    sl = 128 * (k4 % 4)
                    c.step("tensor", [browt, bident], [bpT],
                           lambda q, k4=k4, sl=sl: q.transpose(
                               pT[0:P, sl:sl + 1],
                               rowt[0:1, k4 * P:(k4 + 1) * P], ident[0:1, 0:1]))
                    c.step("scalar", [bpT], [bmodc],
                           lambda q, col=col, sl=sl: q.copy(
                               out=modc[:, col:col + 1], in_=pT[0:P, sl:sl + 1]))
            c.step("vector", [], [bmodc], lambda q: q.tensor_scalar(
                out=modc[:, 8:16], in0=modc[:, 8:16], scalar1=1.0, scalar2=None,
                op0=ALU.add))
            c.step("vector", [bmodc], [bmodcb], lambda q: q.tensor_copy(
                out=modcb[:], in_=modc[:]))

            def clip_row(off):
                c.dma("sync", "d_rowt0", [bhrow], [browt], lambda q: q.dma_start(
                    out=rowt[0:1, 0:D], in_=hrow[0:1, off:off + D]))
                c.step("vector", [browt], [brow2], lambda q: q.tensor_scalar(
                    out=row2[0:1, 0:D], in0=rowt[0:1, 0:D], scalar1=-0.25,
                    scalar2=None, op0=ALU.min))
                c.step("vector", [browt], [bgmsk], lambda q: q.tensor_scalar(
                    out=gmsk[0:1, 0:D], in0=rowt[0:1, 0:D], scalar1=0.0,
                    scalar2=None, op0=ALU.is_ge))
                c.step("vector", [], [browt], lambda q: q.tensor_scalar(
                    out=rowt[0:1, 0:D], in0=rowt[0:1, 0:D], scalar1=0.25,
                    scalar2=None, op0=ALU.max))
                c.step("vector", [brow2], [browt], lambda q: q.tensor_tensor(
                    out=rowt[0:1, 0:D], in0=rowt[0:1, 0:D], in1=row2[0:1, 0:D],
                    op=ALU.subtract))
                c.step("vector", [bgmsk], [browt], lambda q: q.tensor_tensor(
                    out=rowt[0:1, 0:D], in0=rowt[0:1, 0:D], in1=gmsk[0:1, 0:D],
                    op=ALU.mult))
                c.step("vector", [brow2], [browt], lambda q: q.tensor_tensor(
                    out=rowt[0:1, 0:D], in0=rowt[0:1, 0:D], in1=row2[0:1, 0:D],
                    op=ALU.add))

            clip_row(2 * D)
            bcast_seq(rowt, 0, D, ga_r, bga, browt)
            clip_row(5 * D)
            bcast_seq(rowt, 0, D, gb_r, bgb, browt)
            c.dma("sync", "d_rowt0", [bhrow], [browt], lambda q: q.dma_start(
                out=rowt[0:1, 0:D], in_=hrow[0:1, 4 * D:5 * D]))
            bcast_seq(rowt, 0, D, cb1_r, bcb1, browt, scale_add=1.0)
            c.dma("sync", "d_rowt0", [bhrow], [browt], lambda q: q.dma_start(
                out=rowt[0:1, 0:D], in_=hrow[0:1, 3 * D:4 * D]))
            bcast_seq(rowt, 0, D, sb_r, bsb, browt)
            c.dma("sync", "d_rowt0", [], [browt], lambda q: q.dma_start(
                out=rowt[0:1, 0:D], in_=gamma))
            bcast_seq(rowt, 0, D, gam_r, bgam, browt)
            c.dma("sync", "d_rowt0", [], [browt], lambda q: q.dma_start(
                out=rowt[0:1, 0:D], in_=beta))
            bcast_seq(rowt, 0, D, bet_r, bbet, browt)
            c.dma("sync", "d_rowt0", [], [browt], lambda q: q.dma_start(
                out=rowt[0:1, 0:E], in_=brr))
            bcast_seq(rowt, 0, E, br_r, bbrr, browt)
            c.dma("sync", "d_rowt0", [], [browt], lambda q: q.dma_start(
                out=rowt[0:1, 0:2], in_=swin))
            c.step("vector", [browt], [brow2], lambda q: q.tensor_reduce(
                out=row2[0:1, 0:1], in_=rowt[0:1, 0:2], axis=AX.X, op=ALU.max,
                negate=True))
            c.step("scalar", [brow2, browt], [brow2], lambda q: q.activation(
                out=row2[0:1, 2:4], in_=rowt[0:1, 0:2], func=AF.Exp,
                bias=row2[0:1, 0:1], accum_out=row2[0:1, 4:5]))
            c.step("vector", [brow2], [brow2], lambda q: q.reciprocal(
                out=row2[0:1, 5:6], in_=row2[0:1, 4:5]))
            c.step("vector", [brow2], [browt], lambda q: q.tensor_scalar(
                out=rowt[0:1, 0:2], in0=row2[0:1, 2:4], scalar1=row2[0:1, 5:6],
                scalar2=None, op0=ALU.mult))
            bcast_seq(rowt, 0, 2, swb, bswb, browt)

            for ch in range(DCH):
                c.dma("gpsimd", "d_wqv", [], [bwqv], lambda q, ch=ch: q.dma_start(
                    out=wqv_s[:, ch * 2 * D:(ch + 1) * 2 * D],
                    in_=w_qv[ch * P:(ch + 1) * P, :]))
            for j in range(4):
                def f(q, j=j):
                    for ch in range(DCH):
                        last = q.matmul(
                            pS[0:1, 0:512], modcb[:, ch:ch + 1],
                            wqv_s[:, ch * 2 * D + 512 * j:ch * 2 * D + 512 * (j + 1)],
                            start=(ch == 0), stop=(ch == DCH - 1))
                    return last
                c.step("tensor", [bmodcb, bwqv], [bpS], f)
                c.step("vector", [bpS], [bqvb], lambda q, j=j: q.tensor_copy(
                    out=qvb[0:1, 512 * j:512 * (j + 1)], in_=pS[0:1, 0:512]))
            for ch in range(DCH):
                c.step("vector", [bmodc], [bwqv], lambda q, ch=ch: q.tensor_scalar(
                    out=wqv_s[:, ch * 2 * D:(ch + 1) * 2 * D],
                    in0=wqv_s[:, ch * 2 * D:(ch + 1) * 2 * D],
                    scalar1=modc[:, 8 + ch:9 + ch], scalar2=None, op0=ALU.mult))

            for ch in range(DCH):
                c.dma("gpsimd", "d_wsm", [], [bw2c], lambda q, ch=ch: q.dma_start(
                    out=w2c_s[:, ch * 248:(ch + 1) * 248], in_=w2c[ch * P:(ch + 1) * P, :]))
                c.dma("gpsimd", "d_wsm", [], [bwr], lambda q, ch=ch: q.dma_start(
                    out=wr_s[:, ch * E:(ch + 1) * E], in_=wr[ch * P:(ch + 1) * P, :]))
                c.dma("gpsimd", "d_wout", [], [bwout], lambda q, ch=ch: q.dma_start(
                    out=wout_s[:, ch * D:(ch + 1) * D], in_=w_out[ch * P:(ch + 1) * P, :]))
            for ch in range(DCH):
                c.step("vector", [bga], [bwout], lambda q, ch=ch: q.tensor_tensor(
                    out=wout_s[:, ch * D:(ch + 1) * D],
                    in0=wout_s[:, ch * D:(ch + 1) * D], in1=ga_r[:], op=ALU.mult))

            c.dma("sync", "d_misc", [], [bb1t], lambda q: q.dma_start(
                out=b1t[:], in_=b1))
            c.step("scalar", [bb1t], [brb1t], lambda q: q.activation(
                out=rb1t[:], in_=b1t[:], func=AF.Relu))
            

            c.barrier()
            p0_stack.close()

            NXR = 5
            NVR = 4
            x_r = [sa_(f"x_r{j}", [P, D]) for j in range(NXR)]
            bx_r = [Buf(f"x_r{j}") for j in range(NXR)]
            v_r = [sa_(f"v_r{j}", [P, D], BF16) for j in range(NVR)]
            bv_r = [Buf(f"v_r{j}") for j in range(NVR)]
            st6 = sa_("st6", [P, 12]); bst6 = Buf("st6")
            mv2 = sa_("mv2", [P, 2]); bmv2 = Buf("mv2")
            rstd = sa_("rstd", [P, 1]); brstd = Buf("rstd")
            lnx = sa_("lnx", [P, D]); blnx = Buf("lnx")
            lnxT = sa_("lnxT", [P, D], BF16); blnxT = Buf("lnxT")
            rq = [sa_(f"rq{j}", [P, D]) for j in range(2)]
            brq = [Buf(f"rq{j}") for j in range(2)]
            rqT = sa_("rqT", [P, D], BF16); brqT = Buf("rqT")
            lg = sa_("lg", [P, 248]); blg = Buf("lg")
            nmx = sa_("nmx", [P, 8]); bnmx = Buf("nmx")
            ssm = sa_("ssm", [P, 16]); bssm = Buf("ssm")
            asc = sa_("asc", [P, 248], BF16); basc = Buf("asc")
            A41s = [sa_(f"A41s{j}", [P, NHEAD * K41]) for j in range(2)]
            bA41 = [Buf(f"A41s{j}") for j in range(2)]
            A21s = [sa_(f"A21s{j}", [P, NHEAD * K21]) for j in range(2)]
            bA21 = [Buf(f"A21s{j}") for j in range(2)]
            at41 = sa_("at41", [P, NHEAD * P], BF16); bat41 = Buf("at41")
            atb41 = sa_("atb41", [40, NHEAD * P], BF16); batb41 = Buf("atb41")
            at21 = sa_("at21", [P, NHEAD * P], BF16); bat21 = Buf("at21")
            atb21 = sa_("atb21", [20, NHEAD * P], BF16); batb21 = Buf("atb21")
            vw41A = [sa_(f"vw41A{j}", [P, D], BF16) for j in range(2)]
            bvw41A = [Buf(f"vw41A{j}") for j in range(2)]
            vw41B = [sa_(f"vw41B{j}", [40, D], BF16) for j in range(2)]
            bvw41B = [Buf(f"vw41B{j}") for j in range(2)]
            vw21A = [sa_(f"vw21A{j}", [P, D], BF16) for j in range(2)]
            bvw21A = [Buf(f"vw21A{j}") for j in range(2)]
            vw21B = [sa_(f"vw21B{j}", [20, D], BF16) for j in range(2)]
            bvw21B = [Buf(f"vw21B{j}") for j in range(2)]
            aout = sa_("aout", [P, D]); baout = Buf("aout")
            aoutT = sa_("aoutT", [P, D], BF16); baoutT = Buf("aoutT")
            x2t = [sa_(f"x2t{j}", [P, D]) for j in range(2)]
            bx2t = [Buf(f"x2t{j}") for j in range(2)]
            h2t = [sa_(f"h2t{j}", [P, D]) for j in range(2)]
            bh2t = [Buf(f"h2t{j}") for j in range(2)]
            st6h = sa_("st6h", [P, 12]); bst6h = Buf("st6h")
            st6x = sa_("st6x", [P, 12]); bst6x = Buf("st6x")
            mv2h = [sa_(f"mv2h{j}", [P, 2]) for j in range(2)]
            bmv2h = [Buf(f"mv2h{j}") for j in range(2)]
            mv2x = [sa_(f"mv2x{j}", [P, 2]) for j in range(2)]
            bmv2x = [Buf(f"mv2x{j}") for j in range(2)]
            rstdh = sa_("rstdh", [P, 1]); brstdh = Buf("rstdh")
            rstdx = sa_("rstdx", [P, 1]); brstdx = Buf("rstdx")
            rn4a = [sa_(f"rn4a{j}", [P, 1]) for j in range(2)]
            brn4a = [Buf(f"rn4a{j}") for j in range(2)]
            xnb = [sa_(f"xnb{j}", [P, D]) for j in range(2)]
            bxnb = [Buf(f"xnb{j}") for j in range(2)]
            prob = [sa_(f"prob{j}", [P, E]) for j in range(2)]
            bprob = [Buf(f"prob{j}") for j in range(2)]
            t8 = sa_("t8", [P, 8]); bt8 = Buf("t8")
            ge2 = sa_("ge2", [P, E]); bge2 = Buf("ge2")
            ts1 = sa_("ts1", [P, E]); bts1 = Buf("ts1")
            ts2 = sa_("ts2", [P, E]); bts2 = Buf("ts2")
            wa = [sa_(f"wa{j}", [P, 16]) for j in range(2)]
            bwa = [Buf(f"wa{j}") for j in range(2)]
            xntT = sa_("xntT", [P, D], BF16); bxntT = Buf("xntT")
            rn4 = sa_("rn4", [P, 4]); brn4 = Buf("rn4")
            bpSr = [Buf("pSr0"), Buf("pSr1")]
            wstgb = sa_("wstgb", [16, P], BF16); bwstgb = Buf("wstgb")
            wstgf = sa_("wstgf", [16, P]); bwstgf = Buf("wstgf")

            for j_ in range(2):
                c.step("gpsimd", [], [bwa[j_]], lambda q, j_=j_: q.memset(wa[j_][:], 0.0))

            def xslot(t):
                return (t + 1) % NXR

            def vslot(t):
                return (t + 1) % NVR

            def ln_chain(src_t, src_b, out_t, out_b):
                c.step("vector", [src_b], [bst6], lambda q: q.bn_stats(
                    out=st6[:, 0:6], in_=src_t[:, 0:512]))
                c.step("vector", [src_b], [bst6], lambda q: q.bn_stats(
                    out=st6[:, 6:12], in_=src_t[:, 512:1024]))
                c.step("vector", [bst6], [bmv2], lambda q: q.bn_aggr(
                    out=mv2[:], in_=st6[:]))
                c.step("scalar", [bmv2, bconst], [brstd], lambda q: q.activation(
                    out=rstd[:], in_=mv2[:, 1:2], func=AF.Ln, bias=eps_t[:], scale=1.0))
                c.step("scalar", [], [brstd], lambda q: q.activation(
                    out=rstd[:], in_=rstd[:], func=AF.Exp, scale=-0.5))
                c.step("vector", [src_b, bmv2, brstd], [out_b],
                       lambda q: q.tensor_scalar(
                           out=out_t[:], in0=src_t[:], scalar1=mv2[:, 0:1],
                           scalar2=rstd[:], op0=ALU.subtract, op1=ALU.mult))

            def tr8(src_t, src_b, dst_t, dst_b):
                for g in range(2):
                    pd, po, bpd = (pT, 0, bpT) if g == 0 else (pC, 512, bpChi)
                    def tf(q, g=g, pd=pd, po=po):
                        for k in range(4):
                            ch = g * 4 + k
                            last = q.transpose(pd[:, po + 128 * k:po + 128 * k + P],
                                               src_t[:, ch * P:(ch + 1) * P], ident[:])
                        return last
                    c.step("tensor", [src_b, bident], [bpd], tf)
                    c.step("scalar", [bpd], [dst_b], lambda q, g=g, pd=pd, po=po: q.copy(
                        out=dst_t[:, g * 512:(g + 1) * 512], in_=pd[:, po:po + 512]))

            def load_x(t):
                sl = xslot(t)
                c.dma("sync", f"d_x{sl}", [bxh], [bx_r[sl]],
                      lambda q: q.dma_start(out=x_r[sl][:],
                                            in_=xh[(t + 1) * P:(t + 2) * P, :]))

            def ln_part(t):
                sl = xslot(t)
                ln_chain(x_r[sl], bx_r[sl], lnx, blnx)

            def cv_rest(t):
                vsl = vslot(t)
                tr8(lnx, blnx, lnxT, blnxT)
                def vmm(q):
                    q.matmul(pA[:, 0:512], ones1b[:], qvb[0:1, D:D + 512],
                             start=True, stop=False)
                    q.matmul(pA[:, 512:1024], ones1b[:], qvb[0:1, D + 512:2 * D],
                             start=True, stop=False)
                    for ch in range(DCH):
                        fin = ch == DCH - 1
                        q.matmul(pA[:, 0:512], lnxT[:, ch * P:(ch + 1) * P],
                                 wqv_s[:, ch * 2 * D + D:ch * 2 * D + D + 512],
                                 start=False, stop=fin)
                        last = q.matmul(pA[:, 512:1024], lnxT[:, ch * P:(ch + 1) * P],
                                        wqv_s[:, ch * 2 * D + D + 512:ch * 2 * D + 2 * D],
                                        start=False, stop=fin)
                    return last
                c.step("tensor", [blnxT, bqvb, bwqv, bconst], [bpA], vmm)
                if t == -1:
                    c.step("scalar", [bpA, bmasks], [bv_r[vsl]], lambda q: q.activation(
                        out=v_r[vsl][:], in_=pA[:], func=AF.Copy, scale=masks[:, 0:1]))
                elif t == NT:
                    c.step("scalar", [bpA, bmasks], [bv_r[vsl]], lambda q: q.activation(
                        out=v_r[vsl][:], in_=pA[:], func=AF.Copy, scale=masks[:, 1:2]))
                else:
                    c.step("scalar", [bpA], [bv_r[vsl]], lambda q: q.copy(
                        out=v_r[vsl][:], in_=pA[:]))
                if 0 <= t < NT:
                    def qmm(q):
                        q.matmul(pA[:, 0:512], ones1b[:], qvb[0:1, 0:512],
                                 start=True, stop=False)
                        q.matmul(pA[:, 512:1024], ones1b[:], qvb[0:1, 512:D],
                                 start=True, stop=False)
                        for ch in range(DCH):
                            fin = ch == DCH - 1
                            q.matmul(pA[:, 0:512], lnxT[:, ch * P:(ch + 1) * P],
                                     wqv_s[:, ch * 2 * D:ch * 2 * D + 512],
                                     start=False, stop=fin)
                            last = q.matmul(pA[:, 512:1024], lnxT[:, ch * P:(ch + 1) * P],
                                            wqv_s[:, ch * 2 * D + 512:ch * 2 * D + D],
                                            start=False, stop=fin)
                        return last
                    c.step("tensor", [blnxT, bqvb, bwqv, bconst], [bpA], qmm)
                    c.step("scalar", [bpA], [brq[t % 2]], lambda q: q.activation(
                        out=rq[t % 2][:], in_=pA[:], func=AF.Relu))

            def front_main(i):
                pa = i % 2
                tr8(rq[i % 2], brq[i % 2], rqT, brqT)
                def lmm(q):
                    for ch in range(DCH):
                        last = q.matmul(pS[:, 0:248], rqT[:, ch * P:(ch + 1) * P],
                                        w2c_s[:, ch * 248:(ch + 1) * 248],
                                        start=(ch == 0), stop=(ch == DCH - 1))
                    return last
                c.step("tensor", [brqT, bw2c], [bpS], lmm)
                for (cw, off, nmo) in ((21, 0, 0), (41, 84, 4)):
                    c.step("vector", [bpS], [bnmx], lambda q, cw=cw, off=off, nmo=nmo: q.tensor_reduce(
                        out=nmx[:, nmo:nmo + 4],
                        in_=bass.AP(pS.ap().tensor, off, [[512, P], [cw, 4], [1, cw]]),
                        axis=AX.X, op=ALU.max, negate=True))
                    c.step("vector", [bpS, bnmx], [blg], lambda q, cw=cw, off=off, nmo=nmo: q.tensor_tensor(
                        out=bass.AP(lg.ap().tensor, off, [[248, P], [cw, 4], [1, cw]]),
                        in0=bass.AP(pS.ap().tensor, off, [[512, P], [cw, 4], [1, cw]]),
                        in1=bass.AP(nmx.ap().tensor, nmo, [[8, P], [1, 4], [0, cw]]),
                        op=ALU.add))
                c.step("scalar", [blg], [blg], lambda q: q.activation(
                    out=lg[:], in_=lg[:], func=AF.Exp))
                for (cw, off, swcol, smo) in ((21, 0, 0, 0), (41, 84, 1, 8)):
                    c.step("vector", [blg], [bssm], lambda q, cw=cw, off=off, smo=smo: q.tensor_reduce(
                        out=ssm[:, smo:smo + 4],
                        in_=bass.AP(lg.ap().tensor, off, [[248, P], [cw, 4], [1, cw]]),
                        axis=AX.X, op=ALU.add))
                    c.step("vector", [bssm], [bssm], lambda q, smo=smo: q.reciprocal(
                        out=ssm[:, smo + 4:smo + 8], in_=ssm[:, smo:smo + 4]))
                    c.step("vector", [bssm, bswb], [bssm], lambda q, swcol=swcol, smo=smo: q.tensor_scalar(
                        out=ssm[:, smo + 4:smo + 8], in0=ssm[:, smo + 4:smo + 8],
                        scalar1=swb[:, swcol:swcol + 1],
                        scalar2=None, op0=ALU.mult))
                    c.step("vector", [blg, bssm], [basc], lambda q, cw=cw, off=off, smo=smo: q.tensor_tensor(
                        out=bass.AP(asc.ap().tensor, off, [[248, P], [cw, 4], [1, cw]]),
                        in0=bass.AP(lg.ap().tensor, off, [[248, P], [cw, 4], [1, cw]]),
                        in1=bass.AP(ssm.ap().tensor, smo + 4, [[16, P], [1, 4], [0, cw]]),
                        op=ALU.mult))
                c.dma("sync", "d_bw", [basc], [bfb41[pa]], lambda q: q.dma_start(
                    out=bass.AP(fb41[pa], 0, [[PH41, P], [P * PH41, 4], [1, 41]]),
                    in_=bass.AP(asc.ap().tensor, 84, [[248, P], [41, 4], [1, 41]])))
                c.dma("sync", "d_bw", [basc], [bfb21[pa]], lambda q: q.dma_start(
                    out=bass.AP(fb21[pa], 0, [[PH21, P], [P * PH21, 4], [1, 21]]),
                    in_=bass.AP(asc.ap().tensor, 0, [[248, P], [21, 4], [1, 21]])))
                c.dma("gpsimd", "d_br", [bfb41[pa]], [bA41[pa]], lambda q: q.dma_start(
                    out=A41s[pa].ap().rearrange("p (h k) -> p h k", h=4),
                    in_=bass.AP(fb41[pa], 0, [[K41, P], [P * PH41, 4], [1, K41]])))
                c.dma("gpsimd", "d_br", [bfb21[pa]], [bA21[pa]], lambda q: q.dma_start(
                    out=A21s[pa].ap().rearrange("p (h k) -> p h k", h=4),
                    in_=bass.AP(fb21[pa], 0, [[K21, P], [P * PH21, 4], [1, K21]])))
            def vwin(i):
                pa = i % 2
                vm1, v0, vp1 = v_r[vslot(i - 1)], v_r[vslot(i)], v_r[vslot(i + 1)]
                bvm1, bv0, bvp1 = bv_r[vslot(i - 1)], bv_r[vslot(i)], bv_r[vslot(i + 1)]
                c.dma("gpsimd", f"d_vwa{pa}", [bvm1], [bvw41A[pa]],
                      lambda q: q.dma_start(out=vw41A[pa][0:20, :], in_=vm1[108:128, :]))
                c.dma("gpsimd", f"d_vwa{pa}", [bv0], [bvw41A[pa]],
                      lambda q: q.dma_start(out=vw41A[pa][20:128, :], in_=v0[0:108, :]))
                c.dma("gpsimd", f"d_vwb{pa}", [bv0], [bvw41B[pa]],
                      lambda q: q.dma_start(out=vw41B[pa][0:20, :], in_=v0[108:128, :]))
                c.dma("gpsimd", f"d_vwb{pa}", [bvp1], [bvw41B[pa]],
                      lambda q: q.dma_start(out=vw41B[pa][20:40, :], in_=vp1[0:20, :]))
                c.dma("gpsimd", f"d_vwc{pa}", [bvm1], [bvw21A[pa]],
                      lambda q: q.dma_start(out=vw21A[pa][0:10, :], in_=vm1[118:128, :]))
                c.dma("gpsimd", f"d_vwc{pa}", [bv0], [bvw21A[pa]],
                      lambda q: q.dma_start(out=vw21A[pa][10:128, :], in_=v0[0:118, :]))
                c.dma("gpsimd", f"d_vwd{pa}", [bv0], [bvw21B[pa]],
                      lambda q: q.dma_start(out=vw21B[pa][0:10, :], in_=v0[118:128, :]))
                c.dma("gpsimd", f"d_vwd{pa}", [bvp1], [bvw21B[pa]],
                      lambda q: q.dma_start(out=vw21B[pa][10:20, :], in_=vp1[0:10, :]))

            def back1(i):
                pa = i % 2
                for (K, As, bAs, atA, batA, atB, batB, nb) in (
                        (K41, A41s[pa], bA41[pa], at41, bat41, atb41, batb41, 40),
                        (K21, A21s[pa], bA21[pa], at21, bat21, atb21, batb21, 20)):
                    def tA(q, K=K, As=As):
                        for h in range(NHEAD):
                            last = q.transpose(pT[:, 128 * h:128 * h + P],
                                               As[:, h * K:h * K + P], ident[:])
                        return last
                    c.step("tensor", [bAs, bident], [bpT], tA)
                    c.step("scalar", [bpT], [batA], lambda q, atA=atA: q.copy(
                        out=atA[:], in_=pT[:, 0:512]))
                    def tB(q, K=K, As=As, nb=nb):
                        for h in range(NHEAD):
                            last = q.transpose(pC[0:nb, 128 * h:128 * h + P],
                                               As[:, h * K + P:h * K + K], ident[:])
                        return last
                    c.step("tensor", [bAs, bident], [bpC], tB)
                    c.step("scalar", [bpC], [batB], lambda q, atB=atB, nb=nb: q.copy(
                        out=atB[0:nb, :], in_=pC[0:nb, 0:512]))
                def bmm(q):
                    for ci, (atA, atB, vwA, vwB, nb) in enumerate((
                            (at41, atb41, vw41A[pa], vw41B[pa], 40),
                            (at21, atb21, vw21A[pa], vw21B[pa], 20))):
                        for h in range(NHEAD):
                            q.matmul(pB[:, h * HD:h * HD + HD],
                                     atA[:, h * P:(h + 1) * P],
                                     vwA[:, h * HD:h * HD + HD],
                                     start=(ci == 0 and h % 2 == 0), stop=False)
                            last = q.matmul(pB[:, h * HD:h * HD + HD],
                                            atB[0:nb, h * P:(h + 1) * P],
                                            vwB[0:nb, h * HD:h * HD + HD],
                                            start=False,
                                            stop=(ci == 1 and h % 2 == 1))
                    return last
                c.step("tensor", [bat41, batb41, bat21, batb21,
                                  bvw41A[pa], bvw41B[pa], bvw21A[pa], bvw21B[pa]],
                       [bpB], bmm)
                c.step("scalar", [bpB], [baout], lambda q: q.copy(
                    out=aout[:], in_=pB[:]))
                tr8(aout, baout, aoutT, baoutT)
                def wmm(q):
                    for ch in range(DCH):
                        q.matmul(pB[:, 0:512], aoutT[:, ch * P:(ch + 1) * P],
                                 wout_s[:, ch * D:ch * D + 512],
                                 start=(ch == 0), stop=(ch == DCH - 1))
                        last = q.matmul(pB[:, 512:1024], aoutT[:, ch * P:(ch + 1) * P],
                                        wout_s[:, ch * D + 512:(ch + 1) * D],
                                        start=(ch == 0), stop=(ch == DCH - 1))
                    return last
                c.step("tensor", [baoutT, bwout], [bpB], wmm)
                sl = xslot(i)
                c.step("vector", [bpB, bx_r[sl]], [bx2t[pa]], lambda q, pa=pa: q.tensor_tensor(
                    out=x2t[pa][:], in0=x_r[sl][:], in1=pB[:], op=ALU.add))
                c.dma("sync", "d_x2h", [bx2t[pa]], [bx2h], lambda q, pa=pa: q.dma_start(
                    out=x2h[i * P:(i + 1) * P, :], in_=x2t[pa][:]))
                back2v_s1(i)

            def back2v_s1(i):
                pa = i % 2
                c.step("vector", [bx2t[pa]], [bst6h], lambda q, pa=pa: q.bn_stats(
                    out=st6h[:, 0:6], in_=x2t[pa][:, 0:512]))
                c.step("vector", [bx2t[pa]], [bst6h], lambda q, pa=pa: q.bn_stats(
                    out=st6h[:, 6:12], in_=x2t[pa][:, 512:1024]))
                c.step("vector", [bst6h], [bmv2h[pa]], lambda q, pa=pa: q.bn_aggr(
                    out=mv2h[pa][:], in_=st6h[:]))

            def back2v_s2(i):
                pa = i % 2
                c.step("scalar", [bmv2h[pa], bconst], [brstdh], lambda q, pa=pa: q.activation(
                    out=rstdh[:], in_=mv2h[pa][:, 1:2], func=AF.Ln, bias=eps_t[:],
                    scale=1.0))
                c.step("scalar", [], [brstdh], lambda q: q.activation(
                    out=rstdh[:], in_=rstdh[:], func=AF.Exp, scale=-0.5))
                c.step("vector", [bx2t[pa], bmv2h[pa], brstdh], [bh2t[pa]],
                       lambda q, pa=pa: q.tensor_scalar(
                           out=h2t[pa][:], in0=x2t[pa][:], scalar1=mv2h[pa][:, 0:1],
                           scalar2=rstdh[:], op0=ALU.subtract, op1=ALU.mult))
                c.step("vector", [bcb1], [bh2t[pa]], lambda q, pa=pa: q.tensor_tensor(
                    out=h2t[pa][:], in0=h2t[pa][:], in1=cb1_r[:], op=ALU.mult))
                c.step("vector", [bsb], [bh2t[pa]], lambda q, pa=pa: q.tensor_tensor(
                    out=h2t[pa][:], in0=h2t[pa][:], in1=sb_r[:], op=ALU.add))
                c.dma("gpsimd", "d_h2h", [bh2t[pa]], [bh2h], lambda q, i=i, pa=pa: q.dma_start(
                    out=h2h[i * P:(i + 1) * P, :], in_=h2t[pa][:]))
                c.step("vector", [bh2t[pa]], [bst6x], lambda q, pa=pa: q.bn_stats(
                    out=st6x[:, 0:6], in_=h2t[pa][:, 0:512]))
                c.step("vector", [bh2t[pa]], [bst6x], lambda q, pa=pa: q.bn_stats(
                    out=st6x[:, 6:12], in_=h2t[pa][:, 512:1024]))
                c.step("vector", [bst6x], [bmv2x[pa]], lambda q, pa=pa: q.bn_aggr(
                    out=mv2x[pa][:], in_=st6x[:]))

            def back2v_s3(i):
                pa = i % 2
                c.step("scalar", [bmv2x[pa], bconst], [brstdx], lambda q, pa=pa: q.activation(
                    out=rstdx[:], in_=mv2x[pa][:, 1:2], func=AF.Ln, bias=eps_t[:],
                    scale=1.0))
                c.step("scalar", [], [brstdx], lambda q: q.activation(
                    out=rstdx[:], in_=rstdx[:], func=AF.Exp, scale=-0.5))
                c.step("vector", [bh2t[pa], bmv2x[pa], brstdx], [bxnb[pa]],
                       lambda q, pa=pa: q.tensor_scalar(
                           out=xnb[pa][:], in0=h2t[pa][:], scalar1=mv2x[pa][:, 0:1],
                           scalar2=rstdx[:], op0=ALU.subtract, op1=ALU.mult))
                c.step("vector", [bgam], [bxnb[pa]], lambda q, pa=pa: q.tensor_tensor(
                    out=xnb[pa][:], in0=xnb[pa][:], in1=gam_r[:], op=ALU.mult))
                c.step("vector", [bbet], [bxnb[pa]], lambda q, pa=pa: q.tensor_tensor(
                    out=xnb[pa][:], in0=xnb[pa][:], in1=bet_r[:], op=ALU.add))

            def back2p(i):
                pa = i % 2
                tr8(xnb[pa], bxnb[pa], xntT, bxntT)
                c.dma("sync", "d_xnth", [bxntT], [bxnth],
                      lambda q, i=i: q.dma_start(
                          out=bass.AP(xnth.tensor, i * P,
                                      [[TCORE, P], [P * TCORE, DCH], [1, P]]),
                          in_=xntT.ap().rearrange("p (ch t) -> p ch t", ch=DCH)))
                def rmm(q, pa=pa):
                    for ch in range(DCH):
                        last = q.matmul(pS[:, 256 + 8 * pa:256 + 8 * pa + E],
                                        xntT[:, ch * P:(ch + 1) * P],
                                        wr_s[:, ch * E:(ch + 1) * E],
                                        start=(ch == 0), stop=(ch == DCH - 1))
                    return last
                c.step("tensor", [bxntT, bwr], [bpSr[pa]], rmm)

            def back3v_a(i):
                pa = i % 2
                c.step("vector", [bpSr[pa], bbrr], [bprob[pa]], lambda q, pa=pa: q.tensor_tensor(
                    out=prob[pa][:], in0=pS[:, 256 + 8 * pa:256 + 8 * pa + E],
                    in1=br_r[:], op=ALU.add))
                c.step("vector", [bprob[pa]], [brn4a[pa]], lambda q, pa=pa: q.tensor_reduce(
                    out=rn4a[pa][:], in_=prob[pa][:], axis=AX.X, op=ALU.max, negate=True))

            def back3v_b(i):
                pa = i % 2
                c.step("scalar", [brn4a[pa]], [bprob[pa], brn4], lambda q, pa=pa: q.activation(
                    out=prob[pa][:], in_=prob[pa][:], func=AF.Exp, bias=rn4a[pa][:, 0:1],
                    accum_out=rn4[:, 1:2]))
                c.step("vector", [bprob[pa]], [brn4], lambda q: q.reciprocal(
                    out=rn4[:, 2:3], in_=rn4[:, 1:2]))
                c.step("vector", [brn4], [bprob[pa]], lambda q: q.tensor_scalar(
                    out=prob[pa][:], in0=prob[pa][:], scalar1=rn4[:, 2:3], scalar2=None,
                    op0=ALU.mult))
                c.step("vector", [bprob[pa]], [bt8], lambda q: q.max(
                    out=t8[:], in_=prob[pa][:]))
                c.step("vector", [bprob[pa], bt8], [bge2], lambda q: q.tensor_scalar(
                    out=ge2[:], in0=prob[pa][:], scalar1=t8[:, 1:2], scalar2=None,
                    op0=ALU.is_ge))
                c.step("vector", [bge2], [bts1], lambda q: q.tensor_copy(
                    out=ts1[:, 0:1], in_=ge2[:, 0:1]))
                c.step("vector", [bge2], [bts1], lambda q: q.tensor_tensor(
                    out=ts1[:, 1:8], in0=ge2[:, 1:8], in1=ge2[:, 0:7], op=ALU.add))
                c.step("vector", [bts1], [bts2], lambda q: q.tensor_copy(
                    out=ts2[:, 0:2], in_=ts1[:, 0:2]))
                c.step("vector", [bts1], [bts2], lambda q: q.tensor_tensor(
                    out=ts2[:, 2:8], in0=ts1[:, 2:8], in1=ts1[:, 0:6], op=ALU.add))
                c.step("vector", [bts2], [bts1], lambda q: q.tensor_copy(
                    out=ts1[:, 0:4], in_=ts2[:, 0:4]))
                c.step("vector", [bts2], [bts1], lambda q: q.tensor_tensor(
                    out=ts1[:, 4:8], in0=ts2[:, 4:8], in1=ts2[:, 0:4], op=ALU.add))
                c.step("vector", [bge2], [bts1], lambda q: q.tensor_tensor(
                    out=ts1[:], in0=ts1[:], in1=ge2[:], op=ALU.subtract))
                c.step("vector", [], [bts1], lambda q: q.tensor_scalar(
                    out=ts1[:], in0=ts1[:], scalar1=1.0, scalar2=None, op0=ALU.is_le))
                c.step("vector", [bge2, bts1], [bts2], lambda q: q.tensor_tensor(
                    out=ts2[:], in0=ge2[:], in1=ts1[:], op=ALU.mult))
                c.step("vector", [bts2, bprob[pa]], [bwa[pa]], lambda q: q.tensor_tensor(
                    out=wa[pa][:, 0:8], in0=ts2[:], in1=prob[pa][:], op=ALU.mult))
                c.step("vector", [bwa[pa]], [bwa[pa]], lambda q: q.tensor_reduce(
                    out=wa[pa][:, 8:9], in_=wa[pa][:, 0:8], axis=AX.X, op=ALU.add))

            def back3p(i):
                pa = i % 2
                c.step("tensor", [bwa[pa], bident], [bpT], lambda q: q.transpose(
                    pT[0:16, 0:P], wa[pa][:], ident[:]))
                c.step("scalar", [bpT], [bwstgb], lambda q: q.copy(
                    out=wstgb[:], in_=pT[0:16, 0:P]))
                c.step("scalar", [bpT], [bwstgf], lambda q: q.copy(
                    out=wstgf[:], in_=pT[0:16, 0:P]))
                c.dma("sync", "d_wshb", [bwstgb], [bwshb], lambda q, i=i: q.dma_start(
                    out=wshb[:, i * P:(i + 1) * P], in_=wstgb[:]))
                c.dma("gpsimd", "d_wshf", [bwstgf], [bwshf], lambda q, i=i: q.dma_start(
                    out=wshf[:, i * P:(i + 1) * P], in_=wstgf[:]))

            conv_jobs = [(e_, q_) for e_ in range(E) for q_ in range(4)]
            nconv = max(2, (len(conv_jobs) + NT + 2) // (NT + 3))

            def emit_conv(n):
                for _ in range(n):
                    if conv_jobs:
                        e_, qtr_ = conv_jobs.pop(0)
                        c.dma("gpsimd", "d_w2b", [], [bw2b],
                              lambda q, e_=e_, qtr_=qtr_: q.dma_start(
                                  out=w2b[e_, qtr_ * (FF // 4):(qtr_ + 1) * (FF // 4), :],
                                  in_=w2[e_, qtr_ * (FF // 4):(qtr_ + 1) * (FF // 4), :]))

            load_x(-1)
            for j in range(-1, NT + 8):
                emit_conv(nconv)
                if j + 1 <= NT:
                    load_x(j + 1)
                if 0 <= j - 1 < NT:
                    front_main(j - 1)
                if j <= NT:
                    ln_part(j)
                if 0 <= j - 3 < NT:
                    back2v_s2(j - 3)
                if 0 <= j - 2 < NT:
                    back1(j - 2)
                if j <= NT:
                    cv_rest(j)
                if 0 <= j - 4 < NT:
                    back2v_s3(j - 4)
                if 0 <= j - 5 < NT:
                    back2p(j - 5)
                if 0 <= j - 6 < NT:
                    back3v_a(j - 6)
                if 0 <= j - 7 < NT:
                    back3v_b(j - 7)
                if 0 <= j - 8 < NT:
                    back3p(j - 8)
                if 0 <= j - 1 < NT:
                    vwin(j - 1)

        emit_conv(len(conv_jobs))
        c.barrier()
        with contextlib.ExitStack() as pb_stack:
            sbb = lambda n, s, dt=F32: pb_stack.enter_context(nc.sbuf_tensor(n, s, dt))
            w1s = [sbb(f"w1s{j}", [P, DCH * HCG * P], BF16) for j in range(2)]
            bw1s = [Buf(f"w1s{j}") for j in range(2)]
            NW2R = 4
            w2r = [sbb(f"w2r{j}", [P, 2 * D], BF16) for j in range(NW2R)]
            bw2r = [Buf(f"w2r{j}") for j in range(NW2R)]
            h1g = sbb("h1g", [P, HC * GT], BF16); bh1g = Buf("h1g")
            accT = sbb("accT", [P, DCH * GT]); baccT = Buf("accT")
            wrep = [sbb(f"wrep{j}", [P, GT]) for j in range(2)]
            bwrep = [Buf(f"wrep{j}") for j in range(2)]
            tmpv = sbb("tmpv", [P, 512]); btmpv = Buf("tmpv")
            acctok = sbb("acctok", [P, D]); bacctok = Buf("acctok")
            x2rd = sbb("x2rd", [P, D]); bx2rd = Buf("x2rd")
            h2rd = sbb("h2rd", [P, D], BF16); bh2rd = Buf("h2rd")
            yt = sbb("yt", [P, D]); byt = Buf("yt")
            rbrow = sbb("rbrow", [1, D]); brbrow = Buf("rbrow")
            b2m = sbb("b2m", [E, D]); bb2m = Buf("b2m")
            xng = sbb("xng", [P, DCH * GT], BF16); bxng = Buf("xng")
            wsfg = sbb("wsfg", [16, GT]); bwsfg = Buf("wsfg")
            wsbg = sbb("wsbg", [16, GT], BF16); bwsbg = Buf("wsbg")
            c.dma("sync", "d_misc", [], [bb2m], lambda q: q.dma_start(
                out=b2m[:], in_=b2))

            bparts = {"loads", "wrep", "w1", "mm1", "mm2", "rb", "k9", "final"}
            for gi in range(NGRP):
                t0 = gi * GRP_T
                if "xng" in bparts or "loads" in bparts:
                    c.dma("sync", "d_xng", [bxnth], [bxng], lambda q: q.dma_start(
                        out=xng.ap().rearrange("p (ch t) -> p ch t", ch=DCH),
                        in_=xnth[:, :, t0 * P:t0 * P + GT].rearrange(
                            "ch p t -> p ch t")))
                if "wsf" in bparts or "loads" in bparts:
                    c.dma("sync", "d_wsfg", [bwshf], [bwsfg], lambda q: q.dma_start(
                        out=wsfg[:], in_=wshf[:, t0 * P:t0 * P + GT]))
                if "wsb" in bparts or "loads" in bparts:
                    c.dma("sync", "d_wsbg", [bwshb], [bwsbg], lambda q: q.dma_start(
                        out=wsbg[:], in_=wshb[:, t0 * P:t0 * P + GT]))
                for e in range(E):
                    def load_w1(s, e=e):
                        c.dma("gpsimd", f"d_w1{s % 2}", [], [bw1s[s % 2]],
                              lambda q, e=e, s=s: q.dma_start(
                                  out=w1s[s % 2][:], in_=w1h[e, s]))
                    if "w1" in bparts:
                        load_w1(0)
                        if NHCG > 1:
                            load_w1(1)
                    we = e % 2
                    for j in range((GT + 511) // 512) if "wrep" in bparts else []:
                        w_ = min(512, GT - 512 * j)
                        c.step("tensor", [bwsfg, bohot], [bpS],
                               lambda q, j=j, w_=w_, e=e: q.matmul(
                                   pS[:, 0:w_], ohot[:, e * P:(e + 1) * P],
                                   wsfg[0:16, 512 * j:512 * j + w_],
                                   start=True, stop=True))
                        c.step("vector", [bpS], [bwrep[we]],
                               lambda q, j=j, w_=w_, we=we: q.tensor_copy(
                                   out=wrep[we][:, 512 * j:512 * j + w_], in_=pS[:, 0:w_]))
                    for hc in range(HC) if "mm1" in bparts else []:
                        s = hc // HCG
                        w1b = bw1s[s % 2]
                        w1t = w1s[s % 2]
                        co = (hc % HCG) * P
                        pd, po, bpd = (pA, 0, bpA) if hc % 2 == 0 else (pB, 0, bpB)
                        def f1(q, hc=hc, w1t=w1t, co=co, pd=pd):
                            for sg in range(NSEG):
                                for ch in range(DCH):
                                    last = q.matmul(
                                        pd[:, SEGW * sg:SEGW * (sg + 1)],
                                        w1t[:, ch * HCG * P + co:ch * HCG * P + co + P],
                                        xng[:, ch * GT + SEGW * sg:ch * GT + SEGW * (sg + 1)],
                                        start=(ch == 0), stop=(ch == DCH - 1))
                            return last
                        c.step("tensor", [w1b, bxng], [bpd], f1)
                        if (hc + 1) % HCG == 0 and hc // HCG + 2 < NHCG:
                            load_w1(hc // HCG + 2)
                        c.step("scalar", [bpd, bb1t], [bh1g],
                               lambda q, hc=hc, pd=pd, e=e: q.activation(
                                   out=h1g[:, hc * GT:(hc + 1) * GT],
                                   in_=pd[:, 0:GT], func=AF.Relu,
                                   bias=b1t[:, e * HC + hc:e * HC + hc + 1]))
                    if "mm2" not in bparts:
                        continue
                    for th in range(NTH):
                        toff = th * THW
                        for hc2 in range(HC // 2):
                            wsl = hc2 % NW2R
                            c.dma("sync", f"d_w2{wsl}", [bw2b], [bw2r[wsl]],
                                  lambda q, e=e, hc2=hc2, wsl=wsl: q.dma_start(
                                      out=w2r[wsl].ap().rearrange("p (a f) -> p a f", a=2),
                                      in_=w2b[e].rearrange("(a p) f -> p a f", p=P)[
                                          :, 2 * hc2:2 * hc2 + 2, :]))
                            def f2(q, hc2=hc2, wsl=wsl, toff=toff):
                                for a in range(2):
                                    hc = 2 * hc2 + a
                                    for oc in range(DCH):
                                        pd, po, _ = mm2bank[oc]
                                        last = q.matmul(
                                            pd[:, po:po + THW],
                                            w2r[wsl][:, a * D + oc * P:a * D + (oc + 1) * P],
                                            h1g[:, hc * GT + toff:hc * GT + toff + THW],
                                            start=(hc == 0), stop=(hc == HC - 1))
                                return last
                            c.step("tensor", [bw2r[wsl], bh1g],
                                   [bpA, bpB, bpC, bpChi, bpS, bpT], f2)
                        for oc in range(DCH):
                            pd, po, bpd = mm2bank[oc]
                            if e == 0:
                                c.step("vector", [bpd, bwrep[we]], [baccT],
                                       lambda q, oc=oc, pd=pd, po=po, toff=toff, we=we:
                                       q.tensor_tensor(
                                           out=accT[:, oc * GT + toff:oc * GT + toff + THW],
                                           in0=pd[:, po:po + THW],
                                           in1=wrep[we][:, toff:toff + THW], op=ALU.mult))
                            else:
                                c.step("vector", [bpd, bwrep[we]], [btmpv],
                                       lambda q, oc=oc, pd=pd, po=po, toff=toff, we=we:
                                       q.tensor_tensor(
                                           out=tmpv[:, 0:THW], in0=pd[:, po:po + THW],
                                           in1=wrep[we][:, toff:toff + THW], op=ALU.mult))
                                c.step("vector", [btmpv], [baccT],
                                       lambda q, oc=oc, toff=toff: q.tensor_tensor(
                                           out=accT[:, oc * GT + toff:oc * GT + toff + THW],
                                           in0=accT[:, oc * GT + toff:oc * GT + toff + THW],
                                           in1=tmpv[:, 0:THW], op=ALU.add))
                    if gi == 0 and "rb" in bparts:
                        for hc2 in range(HC // 2):
                            wsl = hc2 % NW2R
                            c.dma("sync", f"d_w2{wsl}", [bw2b], [bw2r[wsl]],
                                  lambda q, e=e, hc2=hc2, wsl=wsl: q.dma_start(
                                      out=w2r[wsl].ap().rearrange(
                                          "p (a f) -> p a f", a=2),
                                      in_=w2b[e].rearrange("(a p) f -> p a f", p=P)[
                                          :, 2 * hc2:2 * hc2 + 2, :]))
                            def frb(q, hc2=hc2, wsl=wsl, e=e):
                                for a in range(2):
                                    hc = 2 * hc2 + a
                                    for half in range(2):
                                        pd = pS if half == 0 else pT
                                        last = q.matmul(
                                            pd[0:1, 0:512],
                                            rb1t[:, e * HC + hc:e * HC + hc + 1],
                                            w2r[wsl][:, a * D + half * 512:a * D + half * 512 + 512],
                                            start=(hc == 0), stop=(hc == HC - 1))
                                return last
                            c.step("tensor", [bw2r[wsl], brb1t], [bpS, bpT], frb)
                        c.step("vector", [bpS], [brbrow], lambda q: q.tensor_copy(
                            out=rbrow[0:1, 0:512], in_=pS[0:1, 0:512]))
                        c.step("vector", [bpT], [brbrow], lambda q: q.tensor_copy(
                            out=rbrow[0:1, 512:1024], in_=pT[0:1, 0:512]))
                        c.step("vector", [brbrow], [bctrow], lambda q: q.tensor_tensor(
                            out=ctrow[:], in0=ctrow[:], in1=rbrow[:], op=ALU.add))
                        c.step("vector", [brbrow], [browt], lambda q: q.tensor_scalar(
                            out=rowt[0:1, 0:D], in0=rbrow[:], scalar1=-1.0,
                            scalar2=None, op0=ALU.mult))
                        c.dma("gpsimd", "d_bc9", [browt], [bbc9],
                              lambda q, e=e: q.dma_start(
                                  out=bc9[e:e + 1, :], in_=rowt[0:1, 0:D]))
                if gi == 0 and "rb" in bparts:
                    for hf in range(2):
                        c.step("tensor", [bb2m, bconst], [bpS],
                               lambda q, hf=hf: q.matmul(
                                   pS[0:1, 0:512], ones8[:],
                                   b2m[:, hf * 512:(hf + 1) * 512],
                                   start=True, stop=True))
                        c.step("vector", [bpS], [bctrow],
                               lambda q, hf=hf: q.tensor_tensor(
                                   out=ctrow[0:1, hf * 512:(hf + 1) * 512],
                                   in0=ctrow[0:1, hf * 512:(hf + 1) * 512],
                                   in1=pS[0:1, 0:512], op=ALU.add))
                    c.dma("gpsimd", "d_bc9", [bctrow], [bbc9], lambda q: q.dma_start(
                        out=bc9[8:9, :], in_=ctrow[:]))
                for th in range(NTH) if "k9" in bparts else []:
                    toff = th * THW
                    for oc in range(DCH):
                        c.step("tensor", [bbc9, bwsbg], [bpS],
                               lambda q, oc=oc, toff=toff: q.matmul(
                                   pS[:, 0:THW], bc9[0:9, oc * P:(oc + 1) * P],
                                   wsbg[0:9, toff:toff + THW],
                                   start=True, stop=True))
                        c.step("vector", [bpS], [baccT],
                               lambda q, oc=oc, toff=toff: q.tensor_tensor(
                                   out=accT[:, oc * GT + toff:oc * GT + toff + THW],
                                   in0=accT[:, oc * GT + toff:oc * GT + toff + THW],
                                   in1=pS[:, 0:THW], op=ALU.add))
                for ti in range(GRP_T) if "final" in bparts else []:
                    i = t0 + ti
                    c.dma("sync", "d_x2rd", [bx2h], [bx2rd], lambda q, i=i: q.dma_start(
                        out=x2rd[:], in_=x2h[i * P:(i + 1) * P, :]))
                    c.dma("sync", "d_h2rd", [bh2h], [bh2rd], lambda q, i=i: q.dma_start(
                        out=h2rd[:], in_=h2h[i * P:(i + 1) * P, :]))
                    for oc in range(DCH):
                        sl = 128 * (oc % 4)
                        c.step("tensor", [baccT, bident], [bpT],
                               lambda q, oc=oc, ti=ti, sl=sl: q.transpose(
                                   pT[:, sl:sl + P],
                                   accT[:, oc * GT + ti * P:oc * GT + (ti + 1) * P],
                                   ident[:]))
                        c.step("scalar", [bpT], [bacctok],
                               lambda q, oc=oc, sl=sl: q.copy(
                                   out=acctok[:, oc * P:(oc + 1) * P],
                                   in_=pT[:, sl:sl + P]))
                    c.step("vector", [bh2rd], [bacctok], lambda q: q.tensor_tensor(
                        out=acctok[:], in0=acctok[:], in1=h2rd[:], op=ALU.add))
                    c.step("vector", [bgb], [bacctok], lambda q: q.tensor_tensor(
                        out=acctok[:], in0=acctok[:], in1=gb_r[:], op=ALU.mult))
                    c.step("vector", [bacctok, bx2rd], [byt], lambda q: q.tensor_tensor(
                        out=yt[:], in0=acctok[:], in1=x2rd[:], op=ALU.add))
                    c.dma("sync", "d_y", [byt], [by], lambda q, i=i: q.dma_start(
                        out=y[i * P:(i + 1) * P, :], in_=yt[:]))

            c.final_wait("sync", "d_y")

    return nc


def kernel(**inputs):
    x = np.asarray(inputs["L0_attn_query"])[0]
    T = x.shape[0]
    NCORES = 8
    TC = T // NCORES
    NT = TC // P
    nc = build_program(NT=NT, GRP_T=8)

    xpad = np.zeros((T + 2 * P, D), np.float32)
    xpad[P:P + T] = x
    w2cat = np.ascontiguousarray(np.concatenate(
        [np.asarray(inputs["L0_attn_W2_0"]), np.asarray(inputs["L0_attn_W2_1"])], axis=1))
    w1 = np.asarray(inputs["L0_ffn_W1"])
    w1h = np.ascontiguousarray(
        w1.reshape(E, DCH, P, NHCG, HCG * P).transpose(0, 3, 2, 1, 4)
        .reshape(E, NHCG, P, DCH * HCG * P))
    shared = {
        "w_qv": np.ascontiguousarray(inputs["L0_attn_W_qv"]),
        "w2c": w2cat,
        "swin": np.asarray(inputs["L0_attn_scale_weights"]).reshape(1, 2),
        "w_out": np.ascontiguousarray(inputs["L0_attn_W_out"]),
        "gamma": np.asarray(inputs["L0_ffn_gamma"]).reshape(1, D),
        "beta": np.asarray(inputs["L0_ffn_beta"]).reshape(1, D),
        "wr": np.ascontiguousarray(inputs["L0_ffn_Wr"]),
        "brr": np.asarray(inputs["L0_ffn_br"]).reshape(1, E),
        "w1h": w1h,
        "b1": np.ascontiguousarray(np.asarray(inputs["L0_ffn_b1"]).reshape(E, HC, P).transpose(2, 0, 1).reshape(P, E * HC)),
        "w2": np.ascontiguousarray(inputs["L0_ffn_W2"]),
        "b2": np.ascontiguousarray(inputs["L0_ffn_b2"]),
        "cvec": np.asarray(inputs["L0_c"]).reshape(1, D),
        "wmod": np.ascontiguousarray(inputs["L0_wmod"]),
    }
    in_maps = []
    for ci in range(NCORES):
        m = dict(shared)
        m["xh"] = np.ascontiguousarray(xpad[ci * TC:(ci + 1) * TC + 2 * P])
        lr = np.ones((P, 2), np.float32)
        if ci == 0:
            lr[:, 0] = 0.0
        if ci == NCORES - 1:
            lr[:, 1] = 0.0
        m["lrmask"] = lr
        in_maps.append(m)
    res = run_bass_kernel_spmd(nc, in_maps, core_ids=list(range(NCORES)))
    out = np.concatenate([res.results[i]["y"] for i in range(NCORES)], axis=0)
    return out.reshape(1, T, D).astype(np.float32)


# version 32
# speedup vs baseline: 1.0477x; 1.0477x over previous
import contextlib
import numpy as np

import concourse.bass as bass
import concourse.mybir as mybir
from concourse.bass_utils import run_bass_kernel_spmd

F32 = mybir.dt.float32
BF16 = mybir.dt.bfloat16
AF = mybir.ActivationFunctionType
ALU = mybir.AluOpType
AX = mybir.AxisListType

P = 128
D = 1024
DCH = D // P
E = 8
FF = 4096
HC = FF // P
HCG = 4
NHCG = HC // HCG
HD = 256
NHEAD = 4
EPS = 1e-5
K41, K21 = 168, 148
PH41, PH21 = K41 + 1, K21 + 1


class Buf:
    __slots__ = ("name", "w", "readers")

    def __init__(self, name):
        self.name = name
        self.w = None
        self.readers = []


class Ctx:
    def __init__(self, nc, stack):
        self.nc = nc
        self.stack = stack
        self.sems = {}
        self.counts = {}
        self.q = {"tensor": nc.tensor, "vector": nc.vector, "scalar": nc.scalar,
                  "gpsimd": nc.gpsimd, "sync": nc.sync}
        self.waited = {k: {} for k in self.q}
        for k in ("tensor", "vector", "scalar", "gpsimd"):
            self._mksem("e_" + k)

    def _mksem(self, name):
        self.sems[name] = self.stack.enter_context(self.nc.semaphore(name))
        self.counts[name] = 0

    def _wait(self, eng, tok):
        if tok is None:
            return
        sem_name, val = tok
        wm = self.waited[eng]
        if wm.get(sem_name, 0) >= val:
            return
        self.q[eng].wait_ge(self.sems[sem_name], val)
        wm[sem_name] = val

    def _edges(self, eng, reads, writes):
        for b in reads:
            self._wait(eng, b.w)
        for b in writes:
            self._wait(eng, b.w)
            for t in b.readers:
                self._wait(eng, t)

    def _record(self, tok, reads, writes):
        for b in reads:
            b.readers.append(tok)
        for b in writes:
            b.w = tok
            b.readers = []

    def step(self, eng, reads, writes, fn):
        self._edges(eng, reads, writes)
        last = fn(self.q[eng])
        sem_name = "e_" + eng
        self.counts[sem_name] += 1
        last.then_inc(self.sems[sem_name], 1)
        self._record((sem_name, self.counts[sem_name]), reads, writes)

    def dma(self, qeng, sem_name, reads, writes, fn):
        sem_name = "d_" + writes[0].name
        if sem_name not in self.sems:
            self._mksem(sem_name)
        self._edges(qeng, reads, writes)
        inst = fn(self.q[qeng])
        self.counts[sem_name] += 16
        inst.then_inc(self.sems[sem_name], 16)
        self._record((sem_name, self.counts[sem_name]), reads, writes)

    def final_wait(self, eng, sem_name):
        if sem_name in self.sems:
            self.q[eng].wait_ge(self.sems[sem_name], self.counts[sem_name])
        else:
            for sname, val in self.counts.items():
                if val > 0:
                    self.q[eng].wait_ge(self.sems[sname], val)

    def barrier(self):
        for eng in self.q:
            wm = self.waited[eng]
            for sname, val in self.counts.items():
                if val > 0 and wm.get(sname, 0) < val:
                    self.q[eng].wait_ge(self.sems[sname], val)
                    wm[sname] = val


def build_program(NT=16, GRP_T=8, phase=None):
    TCORE = NT * P
    NGRP = NT // GRP_T
    assert NT % GRP_T == 0
    GT = GRP_T * P
    NTH = max(1, GT // 512)
    THW = GT // NTH
    NSEG = max(1, GT // 512)
    SEGW = GT // NSEG

    nc = bass.Bass("TRN2", target_bir_lowering=False)

    xh = nc.dram_tensor("xh", [(NT + 2) * P, D], F32, kind="ExternalInput").ap()
    lrmask = nc.dram_tensor("lrmask", [P, 2], F32, kind="ExternalInput").ap()
    w_qv = nc.dram_tensor("w_qv", [D, 2 * D], F32, kind="ExternalInput").ap()
    w2c = nc.dram_tensor("w2c", [D, 248], F32, kind="ExternalInput").ap()
    swin = nc.dram_tensor("swin", [1, 2], F32, kind="ExternalInput").ap()
    w_out = nc.dram_tensor("w_out", [D, D], F32, kind="ExternalInput").ap()
    gamma = nc.dram_tensor("gamma", [1, D], F32, kind="ExternalInput").ap()
    beta = nc.dram_tensor("beta", [1, D], F32, kind="ExternalInput").ap()
    wr = nc.dram_tensor("wr", [D, E], F32, kind="ExternalInput").ap()
    brr = nc.dram_tensor("brr", [1, E], F32, kind="ExternalInput").ap()
    w1h = nc.dram_tensor("w1h", [E, NHCG, P, DCH * HCG * P], F32, kind="ExternalInput").ap()
    b1 = nc.dram_tensor("b1", [P, E * HC], F32, kind="ExternalInput").ap()
    w2 = nc.dram_tensor("w2", [E, FF, D], F32, kind="ExternalInput").ap()
    b2 = nc.dram_tensor("b2", [E, D], F32, kind="ExternalInput").ap()
    cvec = nc.dram_tensor("cvec", [1, D], F32, kind="ExternalInput").ap()
    wmod = nc.dram_tensor("wmod", [D, 6 * D], F32, kind="ExternalInput").ap()
    y = nc.dram_tensor("y", [TCORE, D], F32, kind="ExternalOutput").ap()

    fb41 = [nc.dram_tensor(f"fb41_{j}", [NHEAD * P * PH41 + 64], BF16, kind="Internal")
            for j in range(2)]
    fb21 = [nc.dram_tensor(f"fb21_{j}", [NHEAD * P * PH21 + 64], BF16, kind="Internal")
            for j in range(2)]
    w2b = nc.dram_tensor("w2b", [E, FF, D], BF16, kind="Internal").ap()
    x2h = nc.dram_tensor("x2h", [TCORE, D], F32, kind="Internal").ap()
    h2h = nc.dram_tensor("h2h", [TCORE, D], BF16, kind="Internal").ap()
    xnth = nc.dram_tensor("xnth", [DCH, P, TCORE], BF16, kind="Internal").ap()
    hrow = nc.dram_tensor("hrow", [1, 6 * D], F32, kind="Internal").ap()
    wshf = nc.dram_tensor("wshf", [16, TCORE], F32, kind="Internal").ap()
    wshb = nc.dram_tensor("wshb", [16, TCORE], BF16, kind="Internal").ap()

    with contextlib.ExitStack() as stack:
        c = Ctx(nc, stack)
        sb = lambda n, s, dt=F32: stack.enter_context(nc.sbuf_tensor(n, s, dt))
        psd = lambda n, s, dt=F32: stack.enter_context(nc.psum_tensor(n, s, dt))

        pA = psd("pA", [P, 1024]); bpA = Buf("pA")
        pB = psd("pB", [P, 1024]); bpB = Buf("pB")
        pC = psd("pC", [P, 1024]); bpC = Buf("pC")
        pS = psd("pS", [P, 512]); bpS = Buf("pS")
        pT = psd("pT", [P, 512]); bpT = Buf("pT")
        bpChi = Buf("pChi")
        mm2bank = [(pA, 0, bpA), (pA, 512, bpA), (pB, 0, bpB), (pB, 512, bpB),
                   (pC, 0, bpC), (pC, 512, bpC), (pS, 0, bpS), (pT, 0, bpT)]

        ident = sb("ident", [P, P]); bident = Buf("ident")
        eps_t = sb("eps_t", [P, 1])
        ones1 = sb("ones1", [1, P])
        ones1b = sb("ones1b", [1, P], BF16)
        bconst = Buf("const")
        rowt = sb("rowt", [1, D]); browt = Buf("rowt")
        row2 = sb("row2", [1, D]); brow2 = Buf("row2")
        bxnth = Buf("xnth")
        bw2b = Buf("w2b")
        bwshf, bwshb = Buf("wshf"), Buf("wshb")
        bc9 = sb("bc9", [16, D], BF16); bbc9 = Buf("bc9")
        ctrow = sb("ctrow", [1, D]); bctrow = Buf("ctrow")
        b1t = sb("b1t", [P, E * HC]); bb1t = Buf("b1t")
        rb1t = sb("rb1t", [P, E * HC], BF16); brb1t = Buf("rb1t")
        ones8 = sb("ones8", [8, 1])
        ohot = sb("ohot", [16, E * P]); bohot = Buf("ohot")
        gb_r = sb("gb_r", [P, D]); bgb = Buf("gb_r")
        bx2h, bh2h, bxh, by = Buf("x2h"), Buf("h2h"), Buf("xh"), Buf("y")
        bfb41 = [Buf(f"fb41_{j}") for j in range(2)]
        bfb21 = [Buf(f"fb21_{j}") for j in range(2)]

        c.step("gpsimd", [], [bident], lambda q: q.memset(ident[:], 0.0))
        c.step("gpsimd", [], [bident], lambda q: q.affine_select(
            out=ident[:], in_=ident[:], compare_op=ALU.not_equal, fill=1.0,
            base=0, pattern=[[-1, P]], channel_multiplier=1))
        c.step("gpsimd", [], [bconst], lambda q: q.memset(eps_t[:], EPS))
        c.step("gpsimd", [], [bconst], lambda q: q.memset(ones1[:], 1.0))
        c.step("gpsimd", [], [bconst], lambda q: q.memset(ones1b[:], 1.0))
        c.step("gpsimd", [], [bctrow], lambda q: q.memset(ctrow[:], 0.0))
        c.step("gpsimd", [], [bconst], lambda q: q.memset(ones8[:], 1.0))
        c.step("gpsimd", [], [bohot], lambda q: q.memset(ohot[:], 1.0))
        c.step("gpsimd", [], [bohot], lambda q: q.affine_select(
            out=ohot[:], in_=ohot[:], compare_op=ALU.is_equal, fill=0.0,
            base=0, pattern=[[-1, E], [0, P]], channel_multiplier=1))

        def bcast_seq(src_t, src_off, n, dst_t, bdst, bsrc, scale_add=None):
            for j in range((n + 511) // 512):
                w = min(512, n - 512 * j)
                c.step("tensor", [bsrc, bconst], [bpS],
                       lambda q, j=j, w=w: q.matmul(
                           pS[:, 0:w], ones1[:],
                           bass.AP(src_t.ap().tensor, src_off + 512 * j,
                                   [[src_t.shape[1], 1], [1, w]]),
                           start=True, stop=True))
                if scale_add is None:
                    c.step("vector", [bpS], [bdst],
                           lambda q, j=j, w=w: q.tensor_copy(
                               out=dst_t[:, 512 * j:512 * j + w], in_=pS[:, 0:w]))
                else:
                    c.step("vector", [bpS], [bdst],
                           lambda q, j=j, w=w: q.tensor_scalar(
                               out=dst_t[:, 512 * j:512 * j + w], in0=pS[:, 0:w],
                               scalar1=scale_add, scalar2=None, op0=ALU.add))

        with contextlib.ExitStack() as pa_stack:
            sa_ = lambda n, s, dt=F32: pa_stack.enter_context(nc.sbuf_tensor(n, s, dt))
            zb41 = sa_("zb41", [P, 4 * PH41], BF16); bzb = Buf("zb41")
            masks = sa_("masks", [P, 2]); bmasks = Buf("masks")
            modc = sa_("modc", [P, 2 * DCH]); bmodc = Buf("modc")
            modcb = sa_("modcb", [P, 2 * DCH], BF16); bmodcb = Buf("modcb")
            cb1_r = sa_("cb1_r", [P, D]); bcb1 = Buf("cb1_r")
            sb_r = sa_("sb_r", [P, D]); bsb = Buf("sb_r")
            gam_r = sa_("gam_r", [P, D]); bgam = Buf("gam_r")
            bet_r = sa_("bet_r", [P, D]); bbet = Buf("bet_r")
            br_r = sa_("br_r", [P, E]); bbrr = Buf("br_r")
            swb = sa_("swb", [P, 2]); bswb = Buf("swb")
            qvb = sa_("qvb", [1, 2 * D], BF16); bqvb = Buf("qvb")
            wqv_s = sa_("wqv_s", [P, DCH * 2 * D], BF16); bwqv = Buf("wqv_s")
            w2c_s = sa_("w2c_s", [P, DCH * 248], BF16); bw2c = Buf("w2c_s")
            wout_s = sa_("wout_s", [P, DCH * D], BF16); bwout = Buf("wout_s")
            wr_s = sa_("wr_s", [P, DCH * E], BF16); bwr = Buf("wr_s")

            c.step("gpsimd", [], [bzb], lambda q: q.memset(zb41[:], 0.0))
            c.dma("sync", "d_misc", [], [bmasks],
                  lambda q: q.dma_start(out=masks[:], in_=lrmask))
            for j in range(2):
                c.dma("sync", "d_band0", [bzb], [bfb41[j]], lambda q, j=j: q.dma_start(
                    out=bass.AP(fb41[j], 0, [[4 * PH41, P], [1, 4 * PH41]]), in_=zb41[:]))
                c.dma("sync", "d_band0", [bzb], [bfb21[j]], lambda q, j=j: q.dma_start(
                    out=bass.AP(fb21[j], 0, [[4 * PH21, P], [1, 4 * PH21]]),
                    in_=zb41[:, 0:4 * PH21]))

            p0_stack = contextlib.ExitStack()
            s0_ = lambda n, s, dt=F32: p0_stack.enter_context(nc.sbuf_tensor(n, s, dt))
            scT = s0_("scT", [P, DCH], BF16); bscT = Buf("scT")
            scTf = s0_("scTf", [P, DCH]); bscTf = Buf("scTf")
            wmod_s = s0_("wmod_s", [P, 4 * 512], BF16); bwmod = Buf("wmod_s")
            wmod_f = s0_("wmod_f", [P, 4 * 512]); bwmodf = Buf("wmod_f")
            ga_r = s0_("ga_r", [P, D]); bga = Buf("ga_r")
            gmsk = s0_("gmsk", [1, D]); bgmsk = Buf("gmsk")
            stg512 = s0_("stg512", [1, 512]); bstg = Buf("stg512")
            c.dma("sync", "d_rowt0", [], [browt],
                  lambda q: q.dma_start(out=rowt[0:1, 0:D], in_=cvec))
            c.step("scalar", [browt], [brow2], lambda q: q.activation(
                out=row2[0:1, 0:D], in_=rowt[0:1, 0:D], func=AF.Exp, scale=-1.0))
            c.step("vector", [], [brow2], lambda q: q.tensor_scalar(
                out=row2[0:1, 0:D], in0=row2[0:1, 0:D], scalar1=1.0, scalar2=None,
                op0=ALU.add))
            c.step("vector", [], [brow2], lambda q: q.reciprocal(
                out=row2[0:1, 0:D], in_=row2[0:1, 0:D]))
            c.step("vector", [brow2], [browt], lambda q: q.tensor_tensor(
                out=rowt[0:1, 0:D], in0=rowt[0:1, 0:D], in1=row2[0:1, 0:D], op=ALU.mult))
            bhrow = Buf("hrow")
            c.dma("sync", "d_hrow", [browt], [bhrow], lambda q: q.dma_start(
                out=hrow[0:1, 0:D], in_=rowt[0:1, 0:D]))
            for ch in range(DCH):
                sl = 128 * (ch % 4)
                c.step("tensor", [browt, bident], [bpT],
                       lambda q, ch=ch, sl=sl: q.transpose(
                           pT[0:P, sl:sl + 1], rowt[0:1, ch * P:(ch + 1) * P],
                           ident[0:1, 0:1]))
                c.step("scalar", [bpT], [bscTf],
                       lambda q, ch=ch, sl=sl: q.copy(
                           out=scTf[:, ch:ch + 1], in_=pT[0:P, sl:sl + 1]))
                c.step("vector", [bpT], [bscT],
                       lambda q, ch=ch, sl=sl: q.tensor_copy(
                           out=scT[:, ch:ch + 1], in_=pT[0:P, sl:sl + 1]))
            for j6 in range(12):
                fp32_slot = j6 in (4, 5, 10, 11)
                for half in range(2):
                    if fp32_slot:
                        c.dma("sync", "d_wmodf", [], [bwmodf],
                              lambda q, half=half, j6=j6: q.dma_start(
                                  out=wmod_f.ap().rearrange("p (i f) -> p i f", i=4),
                                  in_=wmod.rearrange("(i p) f -> p i f", p=P)[
                                      :, 4 * half:4 * half + 4, 512 * j6:512 * (j6 + 1)]))
                        def mmodf(q, half=half):
                            for i4 in range(4):
                                last = q.matmul(pS[0:1, 0:512],
                                                scTf[:, 4 * half + i4:4 * half + i4 + 1],
                                                wmod_f[:, 512 * i4:512 * (i4 + 1)],
                                                start=(half == 0 and i4 == 0),
                                                stop=(half == 1 and i4 == 3))
                            return last
                        c.step("tensor", [bscTf, bwmodf], [bpS], mmodf)
                    else:
                        c.dma("gpsimd", "d_wmod", [], [bwmod],
                              lambda q, half=half, j6=j6: q.dma_start(
                                  out=wmod_s.ap().rearrange("p (i f) -> p i f", i=4),
                                  in_=wmod.rearrange("(i p) f -> p i f", p=P)[
                                      :, 4 * half:4 * half + 4, 512 * j6:512 * (j6 + 1)]))
                        def mmod2(q, half=half):
                            for i4 in range(4):
                                last = q.matmul(pS[0:1, 0:512],
                                                scT[:, 4 * half + i4:4 * half + i4 + 1],
                                                wmod_s[:, 512 * i4:512 * (i4 + 1)],
                                                start=(half == 0 and i4 == 0),
                                                stop=(half == 1 and i4 == 3))
                            return last
                        c.step("tensor", [bscT, bwmod], [bpS], mmod2)
                c.step("vector", [bpS], [bstg], lambda q: q.tensor_copy(
                    out=stg512[:], in_=pS[0:1, 0:512]))
                c.dma("sync", "d_hrow", [bstg], [bhrow],
                      lambda q, j6=j6: q.dma_start(
                          out=hrow[0:1, 512 * j6:512 * (j6 + 1)], in_=stg512[:]))
            for j6 in range(4):
                c.dma("sync", "d_rowt0", [bhrow], [browt],
                      lambda q, j6=j6: q.dma_start(
                          out=rowt[0:1, 0:512], in_=hrow[0:1, 512 * j6:512 * (j6 + 1)]))
                for k4 in range(4):
                    col = j6 * 4 + k4
                    sl = 128 * (k4 % 4)
                    c.step("tensor", [browt, bident], [bpT],
                           lambda q, k4=k4, sl=sl: q.transpose(
                               pT[0:P, sl:sl + 1],
                               rowt[0:1, k4 * P:(k4 + 1) * P], ident[0:1, 0:1]))
                    c.step("scalar", [bpT], [bmodc],
                           lambda q, col=col, sl=sl: q.copy(
                               out=modc[:, col:col + 1], in_=pT[0:P, sl:sl + 1]))
            c.step("vector", [], [bmodc], lambda q: q.tensor_scalar(
                out=modc[:, 8:16], in0=modc[:, 8:16], scalar1=1.0, scalar2=None,
                op0=ALU.add))
            c.step("vector", [bmodc], [bmodcb], lambda q: q.tensor_copy(
                out=modcb[:], in_=modc[:]))

            def clip_row(off):
                c.dma("sync", "d_rowt0", [bhrow], [browt], lambda q: q.dma_start(
                    out=rowt[0:1, 0:D], in_=hrow[0:1, off:off + D]))
                c.step("vector", [browt], [brow2], lambda q: q.tensor_scalar(
                    out=row2[0:1, 0:D], in0=rowt[0:1, 0:D], scalar1=-0.25,
                    scalar2=None, op0=ALU.min))
                c.step("vector", [browt], [bgmsk], lambda q: q.tensor_scalar(
                    out=gmsk[0:1, 0:D], in0=rowt[0:1, 0:D], scalar1=0.0,
                    scalar2=None, op0=ALU.is_ge))
                c.step("vector", [], [browt], lambda q: q.tensor_scalar(
                    out=rowt[0:1, 0:D], in0=rowt[0:1, 0:D], scalar1=0.25,
                    scalar2=None, op0=ALU.max))
                c.step("vector", [brow2], [browt], lambda q: q.tensor_tensor(
                    out=rowt[0:1, 0:D], in0=rowt[0:1, 0:D], in1=row2[0:1, 0:D],
                    op=ALU.subtract))
                c.step("vector", [bgmsk], [browt], lambda q: q.tensor_tensor(
                    out=rowt[0:1, 0:D], in0=rowt[0:1, 0:D], in1=gmsk[0:1, 0:D],
                    op=ALU.mult))
                c.step("vector", [brow2], [browt], lambda q: q.tensor_tensor(
                    out=rowt[0:1, 0:D], in0=rowt[0:1, 0:D], in1=row2[0:1, 0:D],
                    op=ALU.add))

            clip_row(2 * D)
            bcast_seq(rowt, 0, D, ga_r, bga, browt)
            clip_row(5 * D)
            bcast_seq(rowt, 0, D, gb_r, bgb, browt)
            c.dma("sync", "d_rowt0", [bhrow], [browt], lambda q: q.dma_start(
                out=rowt[0:1, 0:D], in_=hrow[0:1, 4 * D:5 * D]))
            bcast_seq(rowt, 0, D, cb1_r, bcb1, browt, scale_add=1.0)
            c.dma("sync", "d_rowt0", [bhrow], [browt], lambda q: q.dma_start(
                out=rowt[0:1, 0:D], in_=hrow[0:1, 3 * D:4 * D]))
            bcast_seq(rowt, 0, D, sb_r, bsb, browt)
            c.dma("sync", "d_rowt0", [], [browt], lambda q: q.dma_start(
                out=rowt[0:1, 0:D], in_=gamma))
            bcast_seq(rowt, 0, D, gam_r, bgam, browt)
            c.dma("sync", "d_rowt0", [], [browt], lambda q: q.dma_start(
                out=rowt[0:1, 0:D], in_=beta))
            bcast_seq(rowt, 0, D, bet_r, bbet, browt)
            c.dma("sync", "d_rowt0", [], [browt], lambda q: q.dma_start(
                out=rowt[0:1, 0:E], in_=brr))
            bcast_seq(rowt, 0, E, br_r, bbrr, browt)
            c.dma("sync", "d_rowt0", [], [browt], lambda q: q.dma_start(
                out=rowt[0:1, 0:2], in_=swin))
            c.step("vector", [browt], [brow2], lambda q: q.tensor_reduce(
                out=row2[0:1, 0:1], in_=rowt[0:1, 0:2], axis=AX.X, op=ALU.max,
                negate=True))
            c.step("scalar", [brow2, browt], [brow2], lambda q: q.activation(
                out=row2[0:1, 2:4], in_=rowt[0:1, 0:2], func=AF.Exp,
                bias=row2[0:1, 0:1], accum_out=row2[0:1, 4:5]))
            c.step("vector", [brow2], [brow2], lambda q: q.reciprocal(
                out=row2[0:1, 5:6], in_=row2[0:1, 4:5]))
            c.step("vector", [brow2], [browt], lambda q: q.tensor_scalar(
                out=rowt[0:1, 0:2], in0=row2[0:1, 2:4], scalar1=row2[0:1, 5:6],
                scalar2=None, op0=ALU.mult))
            bcast_seq(rowt, 0, 2, swb, bswb, browt)

            for ch in range(DCH):
                c.dma("gpsimd", "d_wqv", [], [bwqv], lambda q, ch=ch: q.dma_start(
                    out=wqv_s[:, ch * 2 * D:(ch + 1) * 2 * D],
                    in_=w_qv[ch * P:(ch + 1) * P, :]))
            for j in range(4):
                def f(q, j=j):
                    for ch in range(DCH):
                        last = q.matmul(
                            pS[0:1, 0:512], modcb[:, ch:ch + 1],
                            wqv_s[:, ch * 2 * D + 512 * j:ch * 2 * D + 512 * (j + 1)],
                            start=(ch == 0), stop=(ch == DCH - 1))
                    return last
                c.step("tensor", [bmodcb, bwqv], [bpS], f)
                c.step("vector", [bpS], [bqvb], lambda q, j=j: q.tensor_copy(
                    out=qvb[0:1, 512 * j:512 * (j + 1)], in_=pS[0:1, 0:512]))
            for ch in range(DCH):
                c.step("vector", [bmodc], [bwqv], lambda q, ch=ch: q.tensor_scalar(
                    out=wqv_s[:, ch * 2 * D:(ch + 1) * 2 * D],
                    in0=wqv_s[:, ch * 2 * D:(ch + 1) * 2 * D],
                    scalar1=modc[:, 8 + ch:9 + ch], scalar2=None, op0=ALU.mult))

            for ch in range(DCH):
                c.dma("gpsimd", "d_wsm", [], [bw2c], lambda q, ch=ch: q.dma_start(
                    out=w2c_s[:, ch * 248:(ch + 1) * 248], in_=w2c[ch * P:(ch + 1) * P, :]))
                c.dma("gpsimd", "d_wsm", [], [bwr], lambda q, ch=ch: q.dma_start(
                    out=wr_s[:, ch * E:(ch + 1) * E], in_=wr[ch * P:(ch + 1) * P, :]))
                c.dma("gpsimd", "d_wout", [], [bwout], lambda q, ch=ch: q.dma_start(
                    out=wout_s[:, ch * D:(ch + 1) * D], in_=w_out[ch * P:(ch + 1) * P, :]))
            for ch in range(DCH):
                c.step("vector", [bga], [bwout], lambda q, ch=ch: q.tensor_tensor(
                    out=wout_s[:, ch * D:(ch + 1) * D],
                    in0=wout_s[:, ch * D:(ch + 1) * D], in1=ga_r[:], op=ALU.mult))

            c.dma("sync", "d_misc", [], [bb1t], lambda q: q.dma_start(
                out=b1t[:], in_=b1))
            c.step("scalar", [bb1t], [brb1t], lambda q: q.activation(
                out=rb1t[:], in_=b1t[:], func=AF.Relu))
            

            c.barrier()
            p0_stack.close()

            NXR = 5
            NVR = 4
            x_r = [sa_(f"x_r{j}", [P, D]) for j in range(NXR)]
            bx_r = [Buf(f"x_r{j}") for j in range(NXR)]
            v_r = [sa_(f"v_r{j}", [P, D], BF16) for j in range(NVR)]
            bv_r = [Buf(f"v_r{j}") for j in range(NVR)]
            st6 = sa_("st6", [P, 12]); bst6 = Buf("st6")
            mv2 = sa_("mv2", [P, 2]); bmv2 = Buf("mv2")
            rstd = sa_("rstd", [P, 1]); brstd = Buf("rstd")
            lnx = sa_("lnx", [P, D]); blnx = Buf("lnx")
            lnxT = sa_("lnxT", [P, D], BF16); blnxT = Buf("lnxT")
            rq = [sa_(f"rq{j}", [P, D]) for j in range(2)]
            brq = [Buf(f"rq{j}") for j in range(2)]
            rqT = sa_("rqT", [P, D], BF16); brqT = Buf("rqT")
            lg = sa_("lg", [P, 248]); blg = Buf("lg")
            nmx = sa_("nmx", [P, 8]); bnmx = Buf("nmx")
            ssm = sa_("ssm", [P, 16]); bssm = Buf("ssm")
            asc = sa_("asc", [P, 248], BF16); basc = Buf("asc")
            A41s = [sa_(f"A41s{j}", [P, NHEAD * K41]) for j in range(2)]
            bA41 = [Buf(f"A41s{j}") for j in range(2)]
            A21s = [sa_(f"A21s{j}", [P, NHEAD * K21]) for j in range(2)]
            bA21 = [Buf(f"A21s{j}") for j in range(2)]
            at41 = sa_("at41", [P, NHEAD * P], BF16); bat41 = Buf("at41")
            atb41 = sa_("atb41", [40, NHEAD * P], BF16); batb41 = Buf("atb41")
            at21 = sa_("at21", [P, NHEAD * P], BF16); bat21 = Buf("at21")
            atb21 = sa_("atb21", [20, NHEAD * P], BF16); batb21 = Buf("atb21")
            vw41A = [sa_(f"vw41A{j}", [P, D], BF16) for j in range(2)]
            bvw41A = [Buf(f"vw41A{j}") for j in range(2)]
            vw41B = [sa_(f"vw41B{j}", [40, D], BF16) for j in range(2)]
            bvw41B = [Buf(f"vw41B{j}") for j in range(2)]
            vw21A = [sa_(f"vw21A{j}", [P, D], BF16) for j in range(2)]
            bvw21A = [Buf(f"vw21A{j}") for j in range(2)]
            vw21B = [sa_(f"vw21B{j}", [20, D], BF16) for j in range(2)]
            bvw21B = [Buf(f"vw21B{j}") for j in range(2)]
            aout = sa_("aout", [P, D]); baout = Buf("aout")
            aoutT = sa_("aoutT", [P, D], BF16); baoutT = Buf("aoutT")
            x2t = [sa_(f"x2t{j}", [P, D]) for j in range(2)]
            bx2t = [Buf(f"x2t{j}") for j in range(2)]
            h2t = [sa_(f"h2t{j}", [P, D]) for j in range(2)]
            bh2t = [Buf(f"h2t{j}") for j in range(2)]
            st6h = sa_("st6h", [P, 12]); bst6h = Buf("st6h")
            st6x = sa_("st6x", [P, 12]); bst6x = Buf("st6x")
            mv2h = [sa_(f"mv2h{j}", [P, 2]) for j in range(2)]
            bmv2h = [Buf(f"mv2h{j}") for j in range(2)]
            mv2x = [sa_(f"mv2x{j}", [P, 2]) for j in range(2)]
            bmv2x = [Buf(f"mv2x{j}") for j in range(2)]
            rstdh = sa_("rstdh", [P, 1]); brstdh = Buf("rstdh")
            rstdx = sa_("rstdx", [P, 1]); brstdx = Buf("rstdx")
            rn4a = [sa_(f"rn4a{j}", [P, 1]) for j in range(2)]
            brn4a = [Buf(f"rn4a{j}") for j in range(2)]
            xnb = [sa_(f"xnb{j}", [P, D]) for j in range(2)]
            bxnb = [Buf(f"xnb{j}") for j in range(2)]
            prob = [sa_(f"prob{j}", [P, E]) for j in range(2)]
            bprob = [Buf(f"prob{j}") for j in range(2)]
            t8 = sa_("t8", [P, 8]); bt8 = Buf("t8")
            ge2 = sa_("ge2", [P, E]); bge2 = Buf("ge2")
            ts1 = sa_("ts1", [P, E]); bts1 = Buf("ts1")
            ts2 = sa_("ts2", [P, E]); bts2 = Buf("ts2")
            wa = [sa_(f"wa{j}", [P, 16]) for j in range(2)]
            bwa = [Buf(f"wa{j}") for j in range(2)]
            xntT = sa_("xntT", [P, D], BF16); bxntT = Buf("xntT")
            rn4 = sa_("rn4", [P, 4]); brn4 = Buf("rn4")
            bpSr = [Buf("pSr0"), Buf("pSr1")]
            wstgb = sa_("wstgb", [16, P], BF16); bwstgb = Buf("wstgb")
            wstgf = sa_("wstgf", [16, P]); bwstgf = Buf("wstgf")

            for j_ in range(2):
                c.step("gpsimd", [], [bwa[j_]], lambda q, j_=j_: q.memset(wa[j_][:], 0.0))

            def xslot(t):
                return (t + 1) % NXR

            def vslot(t):
                return (t + 1) % NVR

            def ln_chain(src_t, src_b, out_t, out_b):
                c.step("vector", [src_b], [bst6], lambda q: q.bn_stats(
                    out=st6[:, 0:6], in_=src_t[:, 0:512]))
                c.step("vector", [src_b], [bst6], lambda q: q.bn_stats(
                    out=st6[:, 6:12], in_=src_t[:, 512:1024]))
                c.step("vector", [bst6], [bmv2], lambda q: q.bn_aggr(
                    out=mv2[:], in_=st6[:]))
                c.step("scalar", [bmv2, bconst], [brstd], lambda q: q.activation(
                    out=rstd[:], in_=mv2[:, 1:2], func=AF.Ln, bias=eps_t[:], scale=1.0))
                c.step("scalar", [], [brstd], lambda q: q.activation(
                    out=rstd[:], in_=rstd[:], func=AF.Exp, scale=-0.5))
                c.step("vector", [src_b, bmv2, brstd], [out_b],
                       lambda q: q.tensor_scalar(
                           out=out_t[:], in0=src_t[:], scalar1=mv2[:, 0:1],
                           scalar2=rstd[:], op0=ALU.subtract, op1=ALU.mult))

            def tr8(src_t, src_b, dst_t, dst_b):
                for g in range(2):
                    pd, po, bpd = (pT, 0, bpT) if g == 0 else (pC, 512, bpChi)
                    def tf(q, g=g, pd=pd, po=po):
                        for k in range(4):
                            ch = g * 4 + k
                            last = q.transpose(pd[:, po + 128 * k:po + 128 * k + P],
                                               src_t[:, ch * P:(ch + 1) * P], ident[:])
                        return last
                    c.step("tensor", [src_b, bident], [bpd], tf)
                    c.step("scalar", [bpd], [dst_b], lambda q, g=g, pd=pd, po=po: q.copy(
                        out=dst_t[:, g * 512:(g + 1) * 512], in_=pd[:, po:po + 512]))

            def load_x(t):
                sl = xslot(t)
                c.dma("sync", f"d_x{sl}", [bxh], [bx_r[sl]],
                      lambda q: q.dma_start(out=x_r[sl][:],
                                            in_=xh[(t + 1) * P:(t + 2) * P, :]))

            def ln_part(t):
                sl = xslot(t)
                ln_chain(x_r[sl], bx_r[sl], lnx, blnx)

            def cv_rest(t):
                vsl = vslot(t)
                tr8(lnx, blnx, lnxT, blnxT)
                def vmm(q):
                    q.matmul(pA[:, 0:512], ones1b[:], qvb[0:1, D:D + 512],
                             start=True, stop=False)
                    q.matmul(pA[:, 512:1024], ones1b[:], qvb[0:1, D + 512:2 * D],
                             start=True, stop=False)
                    for ch in range(DCH):
                        fin = ch == DCH - 1
                        q.matmul(pA[:, 0:512], lnxT[:, ch * P:(ch + 1) * P],
                                 wqv_s[:, ch * 2 * D + D:ch * 2 * D + D + 512],
                                 start=False, stop=fin)
                        last = q.matmul(pA[:, 512:1024], lnxT[:, ch * P:(ch + 1) * P],
                                        wqv_s[:, ch * 2 * D + D + 512:ch * 2 * D + 2 * D],
                                        start=False, stop=fin)
                    return last
                c.step("tensor", [blnxT, bqvb, bwqv, bconst], [bpA], vmm)
                if t == -1:
                    c.step("scalar", [bpA, bmasks], [bv_r[vsl]], lambda q: q.activation(
                        out=v_r[vsl][:], in_=pA[:], func=AF.Copy, scale=masks[:, 0:1]))
                elif t == NT:
                    c.step("scalar", [bpA, bmasks], [bv_r[vsl]], lambda q: q.activation(
                        out=v_r[vsl][:], in_=pA[:], func=AF.Copy, scale=masks[:, 1:2]))
                else:
                    c.step("scalar", [bpA], [bv_r[vsl]], lambda q: q.copy(
                        out=v_r[vsl][:], in_=pA[:]))
                if 0 <= t < NT:
                    def qmm(q):
                        q.matmul(pA[:, 0:512], ones1b[:], qvb[0:1, 0:512],
                                 start=True, stop=False)
                        q.matmul(pA[:, 512:1024], ones1b[:], qvb[0:1, 512:D],
                                 start=True, stop=False)
                        for ch in range(DCH):
                            fin = ch == DCH - 1
                            q.matmul(pA[:, 0:512], lnxT[:, ch * P:(ch + 1) * P],
                                     wqv_s[:, ch * 2 * D:ch * 2 * D + 512],
                                     start=False, stop=fin)
                            last = q.matmul(pA[:, 512:1024], lnxT[:, ch * P:(ch + 1) * P],
                                            wqv_s[:, ch * 2 * D + 512:ch * 2 * D + D],
                                            start=False, stop=fin)
                        return last
                    c.step("tensor", [blnxT, bqvb, bwqv, bconst], [bpA], qmm)
                    c.step("scalar", [bpA], [brq[t % 2]], lambda q: q.activation(
                        out=rq[t % 2][:], in_=pA[:], func=AF.Relu))

            def front_main(i):
                pa = i % 2
                tr8(rq[i % 2], brq[i % 2], rqT, brqT)
                def lmm(q):
                    for ch in range(DCH):
                        last = q.matmul(pS[:, 0:248], rqT[:, ch * P:(ch + 1) * P],
                                        w2c_s[:, ch * 248:(ch + 1) * 248],
                                        start=(ch == 0), stop=(ch == DCH - 1))
                    return last
                c.step("tensor", [brqT, bw2c], [bpS], lmm)
                for (cw, off, nmo) in ((21, 0, 0), (41, 84, 4)):
                    c.step("vector", [bpS], [bnmx], lambda q, cw=cw, off=off, nmo=nmo: q.tensor_reduce(
                        out=nmx[:, nmo:nmo + 4],
                        in_=bass.AP(pS.ap().tensor, off, [[512, P], [cw, 4], [1, cw]]),
                        axis=AX.X, op=ALU.max, negate=True))
                    c.step("vector", [bpS, bnmx], [blg], lambda q, cw=cw, off=off, nmo=nmo: q.tensor_tensor(
                        out=bass.AP(lg.ap().tensor, off, [[248, P], [cw, 4], [1, cw]]),
                        in0=bass.AP(pS.ap().tensor, off, [[512, P], [cw, 4], [1, cw]]),
                        in1=bass.AP(nmx.ap().tensor, nmo, [[8, P], [1, 4], [0, cw]]),
                        op=ALU.add))
                c.step("scalar", [blg], [blg], lambda q: q.activation(
                    out=lg[:], in_=lg[:], func=AF.Exp))
                for (cw, off, swcol, smo) in ((21, 0, 0, 0), (41, 84, 1, 8)):
                    c.step("vector", [blg], [bssm], lambda q, cw=cw, off=off, smo=smo: q.tensor_reduce(
                        out=ssm[:, smo:smo + 4],
                        in_=bass.AP(lg.ap().tensor, off, [[248, P], [cw, 4], [1, cw]]),
                        axis=AX.X, op=ALU.add))
                    c.step("vector", [bssm], [bssm], lambda q, smo=smo: q.reciprocal(
                        out=ssm[:, smo + 4:smo + 8], in_=ssm[:, smo:smo + 4]))
                    c.step("vector", [bssm, bswb], [bssm], lambda q, swcol=swcol, smo=smo: q.tensor_scalar(
                        out=ssm[:, smo + 4:smo + 8], in0=ssm[:, smo + 4:smo + 8],
                        scalar1=swb[:, swcol:swcol + 1],
                        scalar2=None, op0=ALU.mult))
                    c.step("vector", [blg, bssm], [basc], lambda q, cw=cw, off=off, smo=smo: q.tensor_tensor(
                        out=bass.AP(asc.ap().tensor, off, [[248, P], [cw, 4], [1, cw]]),
                        in0=bass.AP(lg.ap().tensor, off, [[248, P], [cw, 4], [1, cw]]),
                        in1=bass.AP(ssm.ap().tensor, smo + 4, [[16, P], [1, 4], [0, cw]]),
                        op=ALU.mult))
                c.dma("sync", "d_bw", [basc], [bfb41[pa]], lambda q: q.dma_start(
                    out=bass.AP(fb41[pa], 0, [[PH41, P], [P * PH41, 4], [1, 41]]),
                    in_=bass.AP(asc.ap().tensor, 84, [[248, P], [41, 4], [1, 41]])))
                c.dma("sync", "d_bw", [basc], [bfb21[pa]], lambda q: q.dma_start(
                    out=bass.AP(fb21[pa], 0, [[PH21, P], [P * PH21, 4], [1, 21]]),
                    in_=bass.AP(asc.ap().tensor, 0, [[248, P], [21, 4], [1, 21]])))
                c.dma("gpsimd", "d_br", [bfb41[pa]], [bA41[pa]], lambda q: q.dma_start(
                    out=A41s[pa].ap().rearrange("p (h k) -> p h k", h=4),
                    in_=bass.AP(fb41[pa], 0, [[K41, P], [P * PH41, 4], [1, K41]])))
                c.dma("gpsimd", "d_br", [bfb21[pa]], [bA21[pa]], lambda q: q.dma_start(
                    out=A21s[pa].ap().rearrange("p (h k) -> p h k", h=4),
                    in_=bass.AP(fb21[pa], 0, [[K21, P], [P * PH21, 4], [1, K21]])))
            def vwin(i):
                pa = i % 2
                vm1, v0, vp1 = v_r[vslot(i - 1)], v_r[vslot(i)], v_r[vslot(i + 1)]
                bvm1, bv0, bvp1 = bv_r[vslot(i - 1)], bv_r[vslot(i)], bv_r[vslot(i + 1)]
                c.dma("sync", f"d_vwa{pa}", [bvm1], [bvw41A[pa]],
                      lambda q: q.dma_start(out=vw41A[pa][0:20, :], in_=vm1[108:128, :]))
                c.dma("sync", f"d_vwa{pa}", [bv0], [bvw41A[pa]],
                      lambda q: q.dma_start(out=vw41A[pa][20:128, :], in_=v0[0:108, :]))
                c.dma("sync", f"d_vwb{pa}", [bv0], [bvw41B[pa]],
                      lambda q: q.dma_start(out=vw41B[pa][0:20, :], in_=v0[108:128, :]))
                c.dma("sync", f"d_vwb{pa}", [bvp1], [bvw41B[pa]],
                      lambda q: q.dma_start(out=vw41B[pa][20:40, :], in_=vp1[0:20, :]))
                c.dma("sync", f"d_vwc{pa}", [bvm1], [bvw21A[pa]],
                      lambda q: q.dma_start(out=vw21A[pa][0:10, :], in_=vm1[118:128, :]))
                c.dma("sync", f"d_vwc{pa}", [bv0], [bvw21A[pa]],
                      lambda q: q.dma_start(out=vw21A[pa][10:128, :], in_=v0[0:118, :]))
                c.dma("sync", f"d_vwd{pa}", [bv0], [bvw21B[pa]],
                      lambda q: q.dma_start(out=vw21B[pa][0:10, :], in_=v0[118:128, :]))
                c.dma("sync", f"d_vwd{pa}", [bvp1], [bvw21B[pa]],
                      lambda q: q.dma_start(out=vw21B[pa][10:20, :], in_=vp1[0:10, :]))

            def back1(i):
                pa = i % 2
                for (K, As, bAs, atA, batA, atB, batB, nb) in (
                        (K41, A41s[pa], bA41[pa], at41, bat41, atb41, batb41, 40),
                        (K21, A21s[pa], bA21[pa], at21, bat21, atb21, batb21, 20)):
                    def tA(q, K=K, As=As):
                        for h in range(NHEAD):
                            last = q.transpose(pT[:, 128 * h:128 * h + P],
                                               As[:, h * K:h * K + P], ident[:])
                        return last
                    c.step("tensor", [bAs, bident], [bpT], tA)
                    c.step("scalar", [bpT], [batA], lambda q, atA=atA: q.copy(
                        out=atA[:], in_=pT[:, 0:512]))
                    def tB(q, K=K, As=As, nb=nb):
                        for h in range(NHEAD):
                            last = q.transpose(pC[0:nb, 128 * h:128 * h + P],
                                               As[:, h * K + P:h * K + K], ident[:])
                        return last
                    c.step("tensor", [bAs, bident], [bpC], tB)
                    c.step("scalar", [bpC], [batB], lambda q, atB=atB, nb=nb: q.copy(
                        out=atB[0:nb, :], in_=pC[0:nb, 0:512]))
                def bmm(q):
                    for ci, (atA, atB, vwA, vwB, nb) in enumerate((
                            (at41, atb41, vw41A[pa], vw41B[pa], 40),
                            (at21, atb21, vw21A[pa], vw21B[pa], 20))):
                        for h in range(NHEAD):
                            q.matmul(pB[:, h * HD:h * HD + HD],
                                     atA[:, h * P:(h + 1) * P],
                                     vwA[:, h * HD:h * HD + HD],
                                     start=(ci == 0 and h % 2 == 0), stop=False)
                            last = q.matmul(pB[:, h * HD:h * HD + HD],
                                            atB[0:nb, h * P:(h + 1) * P],
                                            vwB[0:nb, h * HD:h * HD + HD],
                                            start=False,
                                            stop=(ci == 1 and h % 2 == 1))
                    return last
                c.step("tensor", [bat41, batb41, bat21, batb21,
                                  bvw41A[pa], bvw41B[pa], bvw21A[pa], bvw21B[pa]],
                       [bpB], bmm)
                c.step("scalar", [bpB], [baout], lambda q: q.copy(
                    out=aout[:], in_=pB[:]))
                tr8(aout, baout, aoutT, baoutT)
                def wmm(q):
                    for ch in range(DCH):
                        q.matmul(pB[:, 0:512], aoutT[:, ch * P:(ch + 1) * P],
                                 wout_s[:, ch * D:ch * D + 512],
                                 start=(ch == 0), stop=(ch == DCH - 1))
                        last = q.matmul(pB[:, 512:1024], aoutT[:, ch * P:(ch + 1) * P],
                                        wout_s[:, ch * D + 512:(ch + 1) * D],
                                        start=(ch == 0), stop=(ch == DCH - 1))
                    return last
                c.step("tensor", [baoutT, bwout], [bpB], wmm)
                sl = xslot(i)
                c.step("vector", [bpB, bx_r[sl]], [bx2t[pa]], lambda q, pa=pa: q.tensor_tensor(
                    out=x2t[pa][:], in0=x_r[sl][:], in1=pB[:], op=ALU.add))
                c.dma("sync", "d_x2h", [bx2t[pa]], [bx2h], lambda q, pa=pa: q.dma_start(
                    out=x2h[i * P:(i + 1) * P, :], in_=x2t[pa][:]))
                back2v_s1(i)

            def back2v_s1(i):
                pa = i % 2
                c.step("vector", [bx2t[pa]], [bst6h], lambda q, pa=pa: q.bn_stats(
                    out=st6h[:, 0:6], in_=x2t[pa][:, 0:512]))
                c.step("vector", [bx2t[pa]], [bst6h], lambda q, pa=pa: q.bn_stats(
                    out=st6h[:, 6:12], in_=x2t[pa][:, 512:1024]))
                c.step("vector", [bst6h], [bmv2h[pa]], lambda q, pa=pa: q.bn_aggr(
                    out=mv2h[pa][:], in_=st6h[:]))

            def back2v_s2(i):
                pa = i % 2
                c.step("scalar", [bmv2h[pa], bconst], [brstdh], lambda q, pa=pa: q.activation(
                    out=rstdh[:], in_=mv2h[pa][:, 1:2], func=AF.Ln, bias=eps_t[:],
                    scale=1.0))
                c.step("scalar", [], [brstdh], lambda q: q.activation(
                    out=rstdh[:], in_=rstdh[:], func=AF.Exp, scale=-0.5))
                c.step("vector", [bx2t[pa], bmv2h[pa], brstdh], [bh2t[pa]],
                       lambda q, pa=pa: q.tensor_scalar(
                           out=h2t[pa][:], in0=x2t[pa][:], scalar1=mv2h[pa][:, 0:1],
                           scalar2=rstdh[:], op0=ALU.subtract, op1=ALU.mult))
                c.step("vector", [bcb1], [bh2t[pa]], lambda q, pa=pa: q.tensor_tensor(
                    out=h2t[pa][:], in0=h2t[pa][:], in1=cb1_r[:], op=ALU.mult))
                c.step("vector", [bsb], [bh2t[pa]], lambda q, pa=pa: q.tensor_tensor(
                    out=h2t[pa][:], in0=h2t[pa][:], in1=sb_r[:], op=ALU.add))
                c.dma("gpsimd", "d_h2h", [bh2t[pa]], [bh2h], lambda q, i=i, pa=pa: q.dma_start(
                    out=h2h[i * P:(i + 1) * P, :], in_=h2t[pa][:]))
                c.step("vector", [bh2t[pa]], [bst6x], lambda q, pa=pa: q.bn_stats(
                    out=st6x[:, 0:6], in_=h2t[pa][:, 0:512]))
                c.step("vector", [bh2t[pa]], [bst6x], lambda q, pa=pa: q.bn_stats(
                    out=st6x[:, 6:12], in_=h2t[pa][:, 512:1024]))
                c.step("vector", [bst6x], [bmv2x[pa]], lambda q, pa=pa: q.bn_aggr(
                    out=mv2x[pa][:], in_=st6x[:]))

            def back2v_s3(i):
                pa = i % 2
                c.step("scalar", [bmv2x[pa], bconst], [brstdx], lambda q, pa=pa: q.activation(
                    out=rstdx[:], in_=mv2x[pa][:, 1:2], func=AF.Ln, bias=eps_t[:],
                    scale=1.0))
                c.step("scalar", [], [brstdx], lambda q: q.activation(
                    out=rstdx[:], in_=rstdx[:], func=AF.Exp, scale=-0.5))
                c.step("vector", [bh2t[pa], bmv2x[pa], brstdx], [bxnb[pa]],
                       lambda q, pa=pa: q.tensor_scalar(
                           out=xnb[pa][:], in0=h2t[pa][:], scalar1=mv2x[pa][:, 0:1],
                           scalar2=rstdx[:], op0=ALU.subtract, op1=ALU.mult))
                c.step("vector", [bgam], [bxnb[pa]], lambda q, pa=pa: q.tensor_tensor(
                    out=xnb[pa][:], in0=xnb[pa][:], in1=gam_r[:], op=ALU.mult))
                c.step("vector", [bbet], [bxnb[pa]], lambda q, pa=pa: q.tensor_tensor(
                    out=xnb[pa][:], in0=xnb[pa][:], in1=bet_r[:], op=ALU.add))

            def back2p(i):
                pa = i % 2
                tr8(xnb[pa], bxnb[pa], xntT, bxntT)
                c.dma("sync", "d_xnth", [bxntT], [bxnth],
                      lambda q, i=i: q.dma_start(
                          out=bass.AP(xnth.tensor, i * P,
                                      [[TCORE, P], [P * TCORE, DCH], [1, P]]),
                          in_=xntT.ap().rearrange("p (ch t) -> p ch t", ch=DCH)))
                def rmm(q, pa=pa):
                    for ch in range(DCH):
                        last = q.matmul(pS[:, 256 + 8 * pa:256 + 8 * pa + E],
                                        xntT[:, ch * P:(ch + 1) * P],
                                        wr_s[:, ch * E:(ch + 1) * E],
                                        start=(ch == 0), stop=(ch == DCH - 1))
                    return last
                c.step("tensor", [bxntT, bwr], [bpSr[pa]], rmm)

            def back3v_a(i):
                pa = i % 2
                c.step("vector", [bpSr[pa], bbrr], [bprob[pa]], lambda q, pa=pa: q.tensor_tensor(
                    out=prob[pa][:], in0=pS[:, 256 + 8 * pa:256 + 8 * pa + E],
                    in1=br_r[:], op=ALU.add))
                c.step("vector", [bprob[pa]], [brn4a[pa]], lambda q, pa=pa: q.tensor_reduce(
                    out=rn4a[pa][:], in_=prob[pa][:], axis=AX.X, op=ALU.max, negate=True))

            def back3v_b(i):
                pa = i % 2
                c.step("scalar", [brn4a[pa]], [bprob[pa], brn4], lambda q, pa=pa: q.activation(
                    out=prob[pa][:], in_=prob[pa][:], func=AF.Exp, bias=rn4a[pa][:, 0:1],
                    accum_out=rn4[:, 1:2]))
                c.step("vector", [bprob[pa]], [brn4], lambda q: q.reciprocal(
                    out=rn4[:, 2:3], in_=rn4[:, 1:2]))
                c.step("vector", [brn4], [bprob[pa]], lambda q: q.tensor_scalar(
                    out=prob[pa][:], in0=prob[pa][:], scalar1=rn4[:, 2:3], scalar2=None,
                    op0=ALU.mult))
                c.step("vector", [bprob[pa]], [bt8], lambda q: q.max(
                    out=t8[:], in_=prob[pa][:]))
                c.step("vector", [bprob[pa], bt8], [bge2], lambda q: q.tensor_scalar(
                    out=ge2[:], in0=prob[pa][:], scalar1=t8[:, 1:2], scalar2=None,
                    op0=ALU.is_ge))
                c.step("vector", [bge2], [bts1], lambda q: q.tensor_copy(
                    out=ts1[:, 0:1], in_=ge2[:, 0:1]))
                c.step("vector", [bge2], [bts1], lambda q: q.tensor_tensor(
                    out=ts1[:, 1:8], in0=ge2[:, 1:8], in1=ge2[:, 0:7], op=ALU.add))
                c.step("vector", [bts1], [bts2], lambda q: q.tensor_copy(
                    out=ts2[:, 0:2], in_=ts1[:, 0:2]))
                c.step("vector", [bts1], [bts2], lambda q: q.tensor_tensor(
                    out=ts2[:, 2:8], in0=ts1[:, 2:8], in1=ts1[:, 0:6], op=ALU.add))
                c.step("vector", [bts2], [bts1], lambda q: q.tensor_copy(
                    out=ts1[:, 0:4], in_=ts2[:, 0:4]))
                c.step("vector", [bts2], [bts1], lambda q: q.tensor_tensor(
                    out=ts1[:, 4:8], in0=ts2[:, 4:8], in1=ts2[:, 0:4], op=ALU.add))
                c.step("vector", [bge2], [bts1], lambda q: q.tensor_tensor(
                    out=ts1[:], in0=ts1[:], in1=ge2[:], op=ALU.subtract))
                c.step("vector", [], [bts1], lambda q: q.tensor_scalar(
                    out=ts1[:], in0=ts1[:], scalar1=1.0, scalar2=None, op0=ALU.is_le))
                c.step("vector", [bge2, bts1], [bts2], lambda q: q.tensor_tensor(
                    out=ts2[:], in0=ge2[:], in1=ts1[:], op=ALU.mult))
                c.step("vector", [bts2, bprob[pa]], [bwa[pa]], lambda q: q.tensor_tensor(
                    out=wa[pa][:, 0:8], in0=ts2[:], in1=prob[pa][:], op=ALU.mult))
                c.step("vector", [bwa[pa]], [bwa[pa]], lambda q: q.tensor_reduce(
                    out=wa[pa][:, 8:9], in_=wa[pa][:, 0:8], axis=AX.X, op=ALU.add))

            def back3p(i):
                pa = i % 2
                c.step("tensor", [bwa[pa], bident], [bpT], lambda q: q.transpose(
                    pT[0:16, 0:P], wa[pa][:], ident[:]))
                c.step("scalar", [bpT], [bwstgb], lambda q: q.copy(
                    out=wstgb[:], in_=pT[0:16, 0:P]))
                c.step("scalar", [bpT], [bwstgf], lambda q: q.copy(
                    out=wstgf[:], in_=pT[0:16, 0:P]))
                c.dma("sync", "d_wshb", [bwstgb], [bwshb], lambda q, i=i: q.dma_start(
                    out=wshb[:, i * P:(i + 1) * P], in_=wstgb[:]))
                c.dma("gpsimd", "d_wshf", [bwstgf], [bwshf], lambda q, i=i: q.dma_start(
                    out=wshf[:, i * P:(i + 1) * P], in_=wstgf[:]))

            conv_jobs = [(e_, q_) for e_ in range(E) for q_ in range(16)]
            nconv = max(2, (len(conv_jobs) + NT + 8) // (NT + 9))

            def emit_conv(n):
                for _ in range(n):
                    if conv_jobs:
                        e_, qtr_ = conv_jobs.pop(0)
                        c.dma("gpsimd", "d_w2b", [], [bw2b],
                              lambda q, e_=e_, qtr_=qtr_: q.dma_start(
                                  out=w2b[e_, qtr_ * (FF // 16):(qtr_ + 1) * (FF // 16), :],
                                  in_=w2[e_, qtr_ * (FF // 16):(qtr_ + 1) * (FF // 16), :]))

            load_x(-1)
            for j in range(-1, NT + 8):
                if j + 1 <= NT:
                    load_x(j + 1)
                if 0 <= j - 1 < NT:
                    front_main(j - 1)
                if j <= NT:
                    ln_part(j)
                if 0 <= j - 3 < NT:
                    back2v_s2(j - 3)
                if 0 <= j - 2 < NT:
                    back1(j - 2)
                if j <= NT:
                    cv_rest(j)
                if 0 <= j - 4 < NT:
                    back2v_s3(j - 4)
                if 0 <= j - 5 < NT:
                    back2p(j - 5)
                if 0 <= j - 6 < NT:
                    back3v_a(j - 6)
                if 0 <= j - 7 < NT:
                    back3v_b(j - 7)
                if 0 <= j - 8 < NT:
                    back3p(j - 8)
                if 0 <= j - 1 < NT:
                    vwin(j - 1)
                emit_conv(nconv)

        emit_conv(len(conv_jobs))
        c.barrier()
        with contextlib.ExitStack() as pb_stack:
            sbb = lambda n, s, dt=F32: pb_stack.enter_context(nc.sbuf_tensor(n, s, dt))
            w1s = [sbb(f"w1s{j}", [P, DCH * HCG * P], BF16) for j in range(2)]
            bw1s = [Buf(f"w1s{j}") for j in range(2)]
            NW2R = 4
            w2r = [sbb(f"w2r{j}", [P, 2 * D], BF16) for j in range(NW2R)]
            bw2r = [Buf(f"w2r{j}") for j in range(NW2R)]
            h1g = sbb("h1g", [P, HC * GT], BF16); bh1g = Buf("h1g")
            accT = sbb("accT", [P, DCH * GT]); baccT = Buf("accT")
            wrep = [sbb(f"wrep{j}", [P, GT]) for j in range(2)]
            bwrep = [Buf(f"wrep{j}") for j in range(2)]
            tmpv = sbb("tmpv", [P, 512]); btmpv = Buf("tmpv")
            acctok = sbb("acctok", [P, D]); bacctok = Buf("acctok")
            x2rd = sbb("x2rd", [P, D]); bx2rd = Buf("x2rd")
            h2rd = sbb("h2rd", [P, D], BF16); bh2rd = Buf("h2rd")
            yt = sbb("yt", [P, D]); byt = Buf("yt")
            rbrow = sbb("rbrow", [1, D]); brbrow = Buf("rbrow")
            b2m = sbb("b2m", [E, D]); bb2m = Buf("b2m")
            xng = sbb("xng", [P, DCH * GT], BF16); bxng = Buf("xng")
            wsfg = sbb("wsfg", [16, GT]); bwsfg = Buf("wsfg")
            wsbg = sbb("wsbg", [16, GT], BF16); bwsbg = Buf("wsbg")
            c.dma("sync", "d_misc", [], [bb2m], lambda q: q.dma_start(
                out=b2m[:], in_=b2))

            bparts = {"loads", "wrep", "w1", "mm1", "mm2", "rb", "k9", "final"}
            for gi in range(NGRP):
                t0 = gi * GRP_T
                if "xng" in bparts or "loads" in bparts:
                    c.dma("sync", "d_xng", [bxnth], [bxng], lambda q: q.dma_start(
                        out=xng.ap().rearrange("p (ch t) -> p ch t", ch=DCH),
                        in_=xnth[:, :, t0 * P:t0 * P + GT].rearrange(
                            "ch p t -> p ch t")))
                if "wsf" in bparts or "loads" in bparts:
                    c.dma("sync", "d_wsfg", [bwshf], [bwsfg], lambda q: q.dma_start(
                        out=wsfg[:], in_=wshf[:, t0 * P:t0 * P + GT]))
                if "wsb" in bparts or "loads" in bparts:
                    c.dma("sync", "d_wsbg", [bwshb], [bwsbg], lambda q: q.dma_start(
                        out=wsbg[:], in_=wshb[:, t0 * P:t0 * P + GT]))
                for e in range(E):
                    def load_w1(s, e=e):
                        c.dma("gpsimd", f"d_w1{s % 2}", [], [bw1s[s % 2]],
                              lambda q, e=e, s=s: q.dma_start(
                                  out=w1s[s % 2][:], in_=w1h[e, s]))
                    if "w1" in bparts:
                        load_w1(0)
                        if NHCG > 1:
                            load_w1(1)
                    we = e % 2
                    for j in range((GT + 511) // 512) if "wrep" in bparts else []:
                        w_ = min(512, GT - 512 * j)
                        c.step("tensor", [bwsfg, bohot], [bpS],
                               lambda q, j=j, w_=w_, e=e: q.matmul(
                                   pS[:, 0:w_], ohot[:, e * P:(e + 1) * P],
                                   wsfg[0:16, 512 * j:512 * j + w_],
                                   start=True, stop=True))
                        c.step("vector", [bpS], [bwrep[we]],
                               lambda q, j=j, w_=w_, we=we: q.tensor_copy(
                                   out=wrep[we][:, 512 * j:512 * j + w_], in_=pS[:, 0:w_]))
                    for hc in range(HC) if "mm1" in bparts else []:
                        s = hc // HCG
                        w1b = bw1s[s % 2]
                        w1t = w1s[s % 2]
                        co = (hc % HCG) * P
                        pd, po, bpd = (pA, 0, bpA) if hc % 2 == 0 else (pB, 0, bpB)
                        def f1(q, hc=hc, w1t=w1t, co=co, pd=pd):
                            for sg in range(NSEG):
                                for ch in range(DCH):
                                    last = q.matmul(
                                        pd[:, SEGW * sg:SEGW * (sg + 1)],
                                        w1t[:, ch * HCG * P + co:ch * HCG * P + co + P],
                                        xng[:, ch * GT + SEGW * sg:ch * GT + SEGW * (sg + 1)],
                                        start=(ch == 0), stop=(ch == DCH - 1))
                            return last
                        c.step("tensor", [w1b, bxng], [bpd], f1)
                        if (hc + 1) % HCG == 0 and hc // HCG + 2 < NHCG:
                            load_w1(hc // HCG + 2)
                        c.step("scalar", [bpd, bb1t], [bh1g],
                               lambda q, hc=hc, pd=pd, e=e: q.activation(
                                   out=h1g[:, hc * GT:(hc + 1) * GT],
                                   in_=pd[:, 0:GT], func=AF.Relu,
                                   bias=b1t[:, e * HC + hc:e * HC + hc + 1]))
                    if "mm2" not in bparts:
                        continue
                    for th in range(NTH):
                        toff = th * THW
                        for hc2 in range(HC // 2):
                            wsl = hc2 % NW2R
                            c.dma("sync", f"d_w2{wsl}", [bw2b], [bw2r[wsl]],
                                  lambda q, e=e, hc2=hc2, wsl=wsl: q.dma_start(
                                      out=w2r[wsl].ap().rearrange("p (a f) -> p a f", a=2),
                                      in_=w2b[e].rearrange("(a p) f -> p a f", p=P)[
                                          :, 2 * hc2:2 * hc2 + 2, :]))
                            def f2(q, hc2=hc2, wsl=wsl, toff=toff):
                                for a in range(2):
                                    hc = 2 * hc2 + a
                                    for oc in range(DCH):
                                        pd, po, _ = mm2bank[oc]
                                        last = q.matmul(
                                            pd[:, po:po + THW],
                                            w2r[wsl][:, a * D + oc * P:a * D + (oc + 1) * P],
                                            h1g[:, hc * GT + toff:hc * GT + toff + THW],
                                            start=(hc == 0), stop=(hc == HC - 1))
                                return last
                            c.step("tensor", [bw2r[wsl], bh1g],
                                   [bpA, bpB, bpC, bpChi, bpS, bpT], f2)
                        for oc in range(DCH):
                            pd, po, bpd = mm2bank[oc]
                            if e == 0:
                                c.step("vector", [bpd, bwrep[we]], [baccT],
                                       lambda q, oc=oc, pd=pd, po=po, toff=toff, we=we:
                                       q.tensor_tensor(
                                           out=accT[:, oc * GT + toff:oc * GT + toff + THW],
                                           in0=pd[:, po:po + THW],
                                           in1=wrep[we][:, toff:toff + THW], op=ALU.mult))
                            else:
                                c.step("vector", [bpd, bwrep[we]], [btmpv],
                                       lambda q, oc=oc, pd=pd, po=po, toff=toff, we=we:
                                       q.tensor_tensor(
                                           out=tmpv[:, 0:THW], in0=pd[:, po:po + THW],
                                           in1=wrep[we][:, toff:toff + THW], op=ALU.mult))
                                c.step("vector", [btmpv], [baccT],
                                       lambda q, oc=oc, toff=toff: q.tensor_tensor(
                                           out=accT[:, oc * GT + toff:oc * GT + toff + THW],
                                           in0=accT[:, oc * GT + toff:oc * GT + toff + THW],
                                           in1=tmpv[:, 0:THW], op=ALU.add))
                    if gi == 0 and "rb" in bparts:
                        for hc2 in range(HC // 2):
                            wsl = hc2 % NW2R
                            c.dma("sync", f"d_w2{wsl}", [bw2b], [bw2r[wsl]],
                                  lambda q, e=e, hc2=hc2, wsl=wsl: q.dma_start(
                                      out=w2r[wsl].ap().rearrange(
                                          "p (a f) -> p a f", a=2),
                                      in_=w2b[e].rearrange("(a p) f -> p a f", p=P)[
                                          :, 2 * hc2:2 * hc2 + 2, :]))
                            def frb(q, hc2=hc2, wsl=wsl, e=e):
                                for a in range(2):
                                    hc = 2 * hc2 + a
                                    for half in range(2):
                                        pd = pS if half == 0 else pT
                                        last = q.matmul(
                                            pd[0:1, 0:512],
                                            rb1t[:, e * HC + hc:e * HC + hc + 1],
                                            w2r[wsl][:, a * D + half * 512:a * D + half * 512 + 512],
                                            start=(hc == 0), stop=(hc == HC - 1))
                                return last
                            c.step("tensor", [bw2r[wsl], brb1t], [bpS, bpT], frb)
                        c.step("vector", [bpS], [brbrow], lambda q: q.tensor_copy(
                            out=rbrow[0:1, 0:512], in_=pS[0:1, 0:512]))
                        c.step("vector", [bpT], [brbrow], lambda q: q.tensor_copy(
                            out=rbrow[0:1, 512:1024], in_=pT[0:1, 0:512]))
                        c.step("vector", [brbrow], [bctrow], lambda q: q.tensor_tensor(
                            out=ctrow[:], in0=ctrow[:], in1=rbrow[:], op=ALU.add))
                        c.step("vector", [brbrow], [browt], lambda q: q.tensor_scalar(
                            out=rowt[0:1, 0:D], in0=rbrow[:], scalar1=-1.0,
                            scalar2=None, op0=ALU.mult))
                        c.dma("gpsimd", "d_bc9", [browt], [bbc9],
                              lambda q, e=e: q.dma_start(
                                  out=bc9[e:e + 1, :], in_=rowt[0:1, 0:D]))
                if gi == 0 and "rb" in bparts:
                    for hf in range(2):
                        c.step("tensor", [bb2m, bconst], [bpS],
                               lambda q, hf=hf: q.matmul(
                                   pS[0:1, 0:512], ones8[:],
                                   b2m[:, hf * 512:(hf + 1) * 512],
                                   start=True, stop=True))
                        c.step("vector", [bpS], [bctrow],
                               lambda q, hf=hf: q.tensor_tensor(
                                   out=ctrow[0:1, hf * 512:(hf + 1) * 512],
                                   in0=ctrow[0:1, hf * 512:(hf + 1) * 512],
                                   in1=pS[0:1, 0:512], op=ALU.add))
                    c.dma("gpsimd", "d_bc9", [bctrow], [bbc9], lambda q: q.dma_start(
                        out=bc9[8:9, :], in_=ctrow[:]))
                for th in range(NTH) if "k9" in bparts else []:
                    toff = th * THW
                    for oc in range(DCH):
                        c.step("tensor", [bbc9, bwsbg], [bpS],
                               lambda q, oc=oc, toff=toff: q.matmul(
                                   pS[:, 0:THW], bc9[0:9, oc * P:(oc + 1) * P],
                                   wsbg[0:9, toff:toff + THW],
                                   start=True, stop=True))
                        c.step("vector", [bpS], [baccT],
                               lambda q, oc=oc, toff=toff: q.tensor_tensor(
                                   out=accT[:, oc * GT + toff:oc * GT + toff + THW],
                                   in0=accT[:, oc * GT + toff:oc * GT + toff + THW],
                                   in1=pS[:, 0:THW], op=ALU.add))
                for ti in range(GRP_T) if "final" in bparts else []:
                    i = t0 + ti
                    c.dma("sync", "d_x2rd", [bx2h], [bx2rd], lambda q, i=i: q.dma_start(
                        out=x2rd[:], in_=x2h[i * P:(i + 1) * P, :]))
                    c.dma("sync", "d_h2rd", [bh2h], [bh2rd], lambda q, i=i: q.dma_start(
                        out=h2rd[:], in_=h2h[i * P:(i + 1) * P, :]))
                    for oc in range(DCH):
                        sl = 128 * (oc % 4)
                        c.step("tensor", [baccT, bident], [bpT],
                               lambda q, oc=oc, ti=ti, sl=sl: q.transpose(
                                   pT[:, sl:sl + P],
                                   accT[:, oc * GT + ti * P:oc * GT + (ti + 1) * P],
                                   ident[:]))
                        c.step("scalar", [bpT], [bacctok],
                               lambda q, oc=oc, sl=sl: q.copy(
                                   out=acctok[:, oc * P:(oc + 1) * P],
                                   in_=pT[:, sl:sl + P]))
                    c.step("vector", [bh2rd], [bacctok], lambda q: q.tensor_tensor(
                        out=acctok[:], in0=acctok[:], in1=h2rd[:], op=ALU.add))
                    c.step("vector", [bgb], [bacctok], lambda q: q.tensor_tensor(
                        out=acctok[:], in0=acctok[:], in1=gb_r[:], op=ALU.mult))
                    c.step("vector", [bacctok, bx2rd], [byt], lambda q: q.tensor_tensor(
                        out=yt[:], in0=acctok[:], in1=x2rd[:], op=ALU.add))
                    c.dma("sync", "d_y", [byt], [by], lambda q, i=i: q.dma_start(
                        out=y[i * P:(i + 1) * P, :], in_=yt[:]))

            c.final_wait("sync", "d_y")

    return nc


def kernel(**inputs):
    x = np.asarray(inputs["L0_attn_query"])[0]
    T = x.shape[0]
    NCORES = 8
    TC = T // NCORES
    NT = TC // P
    nc = build_program(NT=NT, GRP_T=8)

    xpad = np.zeros((T + 2 * P, D), np.float32)
    xpad[P:P + T] = x
    w2cat = np.ascontiguousarray(np.concatenate(
        [np.asarray(inputs["L0_attn_W2_0"]), np.asarray(inputs["L0_attn_W2_1"])], axis=1))
    w1 = np.asarray(inputs["L0_ffn_W1"])
    w1h = np.ascontiguousarray(
        w1.reshape(E, DCH, P, NHCG, HCG * P).transpose(0, 3, 2, 1, 4)
        .reshape(E, NHCG, P, DCH * HCG * P))
    shared = {
        "w_qv": np.ascontiguousarray(inputs["L0_attn_W_qv"]),
        "w2c": w2cat,
        "swin": np.asarray(inputs["L0_attn_scale_weights"]).reshape(1, 2),
        "w_out": np.ascontiguousarray(inputs["L0_attn_W_out"]),
        "gamma": np.asarray(inputs["L0_ffn_gamma"]).reshape(1, D),
        "beta": np.asarray(inputs["L0_ffn_beta"]).reshape(1, D),
        "wr": np.ascontiguousarray(inputs["L0_ffn_Wr"]),
        "brr": np.asarray(inputs["L0_ffn_br"]).reshape(1, E),
        "w1h": w1h,
        "b1": np.ascontiguousarray(np.asarray(inputs["L0_ffn_b1"]).reshape(E, HC, P).transpose(2, 0, 1).reshape(P, E * HC)),
        "w2": np.ascontiguousarray(inputs["L0_ffn_W2"]),
        "b2": np.ascontiguousarray(inputs["L0_ffn_b2"]),
        "cvec": np.asarray(inputs["L0_c"]).reshape(1, D),
        "wmod": np.ascontiguousarray(inputs["L0_wmod"]),
    }
    in_maps = []
    for ci in range(NCORES):
        m = dict(shared)
        m["xh"] = np.ascontiguousarray(xpad[ci * TC:(ci + 1) * TC + 2 * P])
        lr = np.ones((P, 2), np.float32)
        if ci == 0:
            lr[:, 0] = 0.0
        if ci == NCORES - 1:
            lr[:, 1] = 0.0
        m["lrmask"] = lr
        in_maps.append(m)
    res = run_bass_kernel_spmd(nc, in_maps, core_ids=list(range(NCORES)))
    out = np.concatenate([res.results[i]["y"] for i in range(NCORES)], axis=0)
    return out.reshape(1, T, D).astype(np.float32)
